# version 21
# baseline (speedup 1.0000x reference)
import numpy as np

import concourse.bass as bass
import concourse.mybir as mybir
from concourse.bass_utils import run_bass_kernel_spmd

F32 = mybir.dt.float32
BF16 = mybir.dt.bfloat16
I32 = mybir.dt.int32
U8 = mybir.dt.uint8
AF = mybir.ActivationFunctionType
ALU = mybir.AluOpType
AX = mybir.AxisListType

N_CORES = 8
N_TOK = 8192
D = 1024
NT = N_TOK // 128
H = 16
DH = 64
M = 64
SCALE = 0.125
EPS = 1e-5
FH = 4
NK = 64
NEXP = 4096
I_FFN = 4096
BIG = 1e30
MAGIC = 0x5F3759DF

ENGINES = ("sync", "tensor", "vector", "scalar", "gpsimd")
DTSIZE = {F32: 4, BF16: 2, I32: 4, U8: 1}

ARENA_BYTES = 212700
PSUM_BANKS = 8


DMA_POOL = 24


class Prog:

    def __init__(self, nc):
        self.nc = nc
        self.ops = []
        self.last_writer = {}
        self.readers = {}
        self.floor = set()
        self.dma_count = {}

    def op(self, eng, fn, reads=(), writes=(), inc=1):
        idx = len(self.ops)
        deps = set(self.floor)
        for b in reads:
            w = self.last_writer.get(b)
            if w is not None:
                deps.add(w)
        for b in writes:
            w = self.last_writer.get(b)
            if w is not None:
                deps.add(w)
            deps.update(self.readers.get(b, ()))
        if inc == 16:
            n = self.dma_count.get(eng, 0)
            self.dma_count[eng] = n + 1
            sem_id = f"{eng}#{n % DMA_POOL}"
        else:
            sem_id = eng
        self.ops.append([eng, fn, deps, inc, sem_id])
        for b in reads:
            self.readers.setdefault(b, []).append(idx)
        for b in writes:
            self.last_writer[b] = idx
            self.readers[b] = []
        return idx

    def dma(self, q, out, in_, reads, writes, **kw):
        self.op(q, lambda e: e.dma_start(out=out, in_=in_, **kw), reads, writes, inc=16)

    def dmat(self, out, in_, reads, writes):
        self.op("sync", lambda e: e.dma_start(out=out, in_=in_, transpose=True),
                reads, writes, inc=16)

    def dmatb(self, out, in_, chunks, reads, writes):
        self.op("sync", lambda e: e.dma_start(
            out=out.rearrange("p (c f) -> p c f", c=chunks),
            in_=in_.rearrange("p (c f) -> p c f", c=chunks),
            transpose=True), reads, writes, inc=16)

    def mm(self, out, lhsT, rhs, reads, writes, start=True, stop=True):
        self.op("tensor",
                lambda e: e.matmul(out, lhsT, rhs, start=start, stop=stop),
                reads, writes)

    def barrier(self):
        last_by_sem = {}
        for i, rec in enumerate(self.ops):
            last_by_sem[rec[4]] = i
        deps = set(last_by_sem.values()) | self.floor
        new_floor = set()
        for eng in ENGINES:
            idx = len(self.ops)
            self.ops.append([eng, lambda e: e.nop(), set(deps), 1, eng])
            new_floor.add(idx)
        self.floor = new_floor

    def emit(self, block, sems):
        tok = [None] * len(self.ops)
        counts = {}
        for i, (eng, _, _, inc, sem_id) in enumerate(self.ops):
            counts[sem_id] = counts.get(sem_id, 0) + inc
            tok[i] = (sem_id, counts[sem_id])
        per_eng = {e: [] for e in ENGINES}
        for i, rec in enumerate(self.ops):
            per_eng[rec[0]].append(rec)
        self.final_counts = counts

        for eng in ENGINES:
            lst = per_eng[eng]
            if not lst:
                continue

            def body(engine, lst=lst, eng=eng):
                waited = {}
                for eng_, fn, deps, inc, sem_id in lst:
                    need = {}
                    for j in deps:
                        s2, v = tok[j]
                        if s2 == "tensor" and eng == "tensor":
                            continue
                        if v > waited.get(s2, 0):
                            need[s2] = max(need.get(s2, 0), v)
                    for s2 in sorted(need):
                        engine.wait_ge(sems[s2], need[s2])
                        waited[s2] = need[s2]
                    inst = fn(engine)
                    if inst is not None:
                        inst.then_inc(sems[sem_id], inc)

            getattr(block, eng)(body)


class Arena:

    def __init__(self, t):
        self.t = t
        self.off = 0

    def alloc(self, parts, free, dtype, align=64):
        nbytes = free * DTSIZE[dtype]
        self.off = (self.off + align - 1) // align * align
        assert self.off + nbytes <= ARENA_BYTES, \
            f"SBUF arena overflow: {self.off}+{nbytes}"
        ap = self.t[0:parts, self.off:self.off + nbytes].bitcast(dtype)
        self.off += nbytes
        return ap

    def mark(self):
        return self.off

    def release(self, mark):
        self.off = mark


class PsumArena:

    def __init__(self, t):
        self.t = t
        self.bank = 0

    def alloc(self, parts, free_f32):
        banks = (free_f32 + 511) // 512
        assert self.bank + banks <= PSUM_BANKS, "PSUM overflow"
        ap = self.t[0:parts, self.bank * 512:self.bank * 512 + free_f32]
        self.bank += banks
        return ap

    def mark(self):
        return self.bank

    def release(self, mark):
        self.bank = mark



def _build_keysbd(keys):
    kt = np.asarray(keys, dtype=np.float32).transpose(2, 0, 3, 1)
    out = np.zeros((128, 512), dtype=np.float32)
    for pnum in range(2):
        for h in range(4):
            c = pnum * 2 + h // 2
            lo = (h % 2) * 64
            out[lo:lo + 64, c * 128 + lo:c * 128 + lo + 64] = kt[pnum, h]
    return np.ascontiguousarray(out)


def host_prepare(inputs):
    x = np.asarray(inputs["L0_attn_hidden_states"], dtype=np.float32)
    consts = np.zeros((128, 128), dtype=np.float32)
    np.fill_diagonal(consts, 1.0)
    shared = {
        "wqkv": np.ascontiguousarray(inputs["L0_attn_Wqkv"], dtype=np.float32),
        "bqkv": np.ascontiguousarray(
            np.asarray(inputs["L0_attn_bqkv"], dtype=np.float32).reshape(1, 3 * D)),
        "wout": np.ascontiguousarray(inputs["L0_attn_Wout"], dtype=np.float32),
        "bout": np.ascontiguousarray(
            np.asarray(inputs["L0_attn_bout"], dtype=np.float32).reshape(1, D)),
        "wup_t": np.ascontiguousarray(
            np.asarray(inputs["L0_ffn_W_up"], dtype=np.float32).T),
        "wdown_t": np.ascontiguousarray(
            np.asarray(inputs["L0_ffn_W_down"], dtype=np.float32).T),
        "wq_t": np.ascontiguousarray(
            np.asarray(inputs["L0_ffn_W_q"], dtype=np.float32).T),
        "keysbd": _build_keysbd(inputs["L0_ffn_keys"]),
        "de_t": np.ascontiguousarray(
            np.asarray(inputs["L0_ffn_down_embed"], dtype=np.float32).T),
        "ue": np.ascontiguousarray(inputs["L0_ffn_up_embed"], dtype=np.float32),
        "eye128": consts,
    }
    per_core = [{"x0": np.ascontiguousarray(x[c])} for c in range(N_CORES)]
    return shared, per_core



def emit_layernorm(p, ar, x_ap, x_key, n_cols, scratch, scratch_key,
                   outs, tag):
    s1, s2, t1, t2, yi, ii, rstd, nbias = outs
    k = tag
    p.op("vector", lambda e: e.reduce_sum(s1, x_ap, axis=AX.X),
         [x_key], [k + "s1"])
    p.op("scalar", lambda e: e.activation(out=scratch, in_=x_ap, func=AF.Square,
                                          accum_out=s2),
         [x_key], [scratch_key, k + "s2"])
    inv_n = 1.0 / n_cols
    p.op("vector", lambda e: e.tensor_scalar(out=t1, in0=s1, scalar1=inv_n,
                                             scalar2=None, op0=ALU.mult),
         [k + "s1"], [k + "t1"])
    p.op("vector", lambda e: e.tensor_scalar(out=t2, in0=s2, scalar1=inv_n,
                                             scalar2=None, op0=ALU.mult),
         [k + "s2"], [k + "t2"])
    p.op("vector", lambda e: e.tensor_mul(s1, t1, t1), [k + "t1"], [k + "s1"])
    p.op("vector", lambda e: e.tensor_sub(s2, t2, s1),
         [k + "t2", k + "s1"], [k + "s2"])
    p.op("vector", lambda e: e.tensor_scalar(out=s2, in0=s2, scalar1=EPS,
                                             scalar2=None, op0=ALU.add),
         [k + "s2"], [k + "s2"])
    p.op("vector", lambda e: e.tensor_scalar(out=ii, in0=s2.bitcast(I32),
                                             scalar1=1, scalar2=None,
                                             op0=ALU.logical_shift_right),
         [k + "s2"], [k + "ii"])
    p.op("vector", lambda e: e.tensor_scalar(out=ii, in0=ii, scalar1=-1,
                                             scalar2=MAGIC, op0=ALU.mult,
                                             op1=ALU.add),
         [k + "ii"], [k + "ii"])
    p.op("vector", lambda e: e.tensor_copy(out=rstd, in_=ii.bitcast(F32)),
         [k + "ii"], [k + "rstd"])
    p.op("vector", lambda e: e.tensor_scalar(out=yi, in0=s2, scalar1=-0.5,
                                             scalar2=None, op0=ALU.mult),
         [k + "s2"], [k + "yi"])
    for it in range(4):
        p.op("vector", lambda e: e.tensor_mul(t2, rstd, rstd),
             [k + "rstd"], [k + "t2"])
        p.op("vector", lambda e: e.tensor_scalar(out=t2, in0=t2, scalar1=yi,
                                                 scalar2=1.5, op0=ALU.mult,
                                                 op1=ALU.add),
             [k + "t2", k + "yi"], [k + "t2"])
        p.op("vector", lambda e: e.tensor_mul(rstd, rstd, t2),
             [k + "rstd", k + "t2"], [k + "rstd"])
    p.op("vector", lambda e: e.scalar_tensor_tensor(out=nbias, in0=t1,
                                                    scalar=-1.0, in1=rstd,
                                                    op0=ALU.mult, op1=ALU.mult),
         [k + "t1", k + "rstd"], [k + "nbias"])
    return k + "rstd", k + "nbias"



def build(debug=False, stages=99, micro=99):
    nc = bass.Bass("TRN2", target_bir_lowering=False)

    x0_d = nc.dram_tensor("x0", [N_TOK, D], F32, kind="ExternalInput")
    wqkv_d = nc.dram_tensor("wqkv", [D, 3 * D], F32, kind="ExternalInput")
    bqkv_d = nc.dram_tensor("bqkv", [1, 3 * D], F32, kind="ExternalInput")
    wout_d = nc.dram_tensor("wout", [D, D], F32, kind="ExternalInput")
    bout_d = nc.dram_tensor("bout", [1, D], F32, kind="ExternalInput")
    wup_d = nc.dram_tensor("wup_t", [D, I_FFN], F32, kind="ExternalInput")
    wdown_d = nc.dram_tensor("wdown_t", [I_FFN, D], F32, kind="ExternalInput")
    wq_d = nc.dram_tensor("wq_t", [D, 512], F32, kind="ExternalInput")
    keysbd_d = nc.dram_tensor("keysbd", [128, 512], F32, kind="ExternalInput")
    de_d = nc.dram_tensor("de_t", [D, NEXP], F32, kind="ExternalInput")
    ue_d = nc.dram_tensor("ue", [NEXP, D], F32, kind="ExternalInput")
    eye128_d = nc.dram_tensor("eye128", [128, 128], F32, kind="ExternalInput")

    out_d = nc.dram_tensor("out", [N_TOK, D], F32, kind="ExternalOutput")

    skind = dict(kind="ExternalOutput") if debug else {}
    qkv_d = nc.dram_tensor("qkv_s", [N_TOK, 3 * D], BF16, **skind)
    x1_d = nc.dram_tensor("x1_s", [N_TOK, D], F32, **skind)
    dense_d = nc.dram_tensor("dense_s", [N_TOK, D], F32, **skind)
    exm_d = nc.dram_tensor("exm_s", [N_TOK, 512], BF16, **skind)
    sel_d = nc.dram_tensor("sel_s", [N_TOK, 16], F32, **skind)

    dbg = {}

    def dbg_out(name, shape, dt=F32):
        if debug:
            dbg[name] = nc.dram_tensor("dbg_" + name, shape, dt,
                                       kind="ExternalOutput")
        return dbg.get(name)

    dbg_out("qlT", [128, 512])
    dbg_out("klT", [128, 512])
    dbg_out("attn2", [128, 1024])
    dbg_out("pinvT", [128, 1024])
    dbg_out("t3v", [128, 1024])
    dbg_out("inner", [128, 8 * 130], BF16)

    from contextlib import ExitStack
    with ExitStack() as stack:
        sems = {e: stack.enter_context(nc.semaphore(f"S_{e}")) for e in ENGINES}
        for q in ("sync", "scalar", "gpsimd"):
            for j in range(DMA_POOL):
                sems[f"{q}#{j}"] = stack.enter_context(
                    nc.semaphore(f"S_{q}_{j}"))
        arena_t = stack.enter_context(
            nc.sbuf_tensor("arena", [128, ARENA_BYTES], U8))
        psum_t = stack.enter_context(
            nc.psum_tensor("psumA", [128, 4096], F32))
        ar = Arena(arena_t)
        ps = PsumArena(psum_t)
        p = Prog(nc)

        klbd_bf = ar.alloc(128, 1024, BF16)
        inner_bd = ar.alloc(128, 8 * 130, BF16)
        inv128_bf = ar.alloc(128, 1, BF16)
        ones_bf = ar.alloc(128, 1, BF16)
        smalls = [ar.alloc(128, 16, F32) for _ in range(2)]
        smalls_i = [ar.alloc(128, 1, I32) for _ in range(2)]
        lnscr = [ar.alloc(128, 1024, BF16)] * 2

        def _lns(j):
            sm, si = smalls[j], smalls_i[j]
            return dict(
                s1=sm[:, 0:1], s2=sm[:, 1:2], t1=sm[:, 2:3],
                t2=sm[:, 3:4], yi=sm[:, 4:5], rstd=sm[:, 5:6],
                nbias=sm[:, 6:7], ii=si[:, 0:1],
            )
        LNSx = [_lns(0), _lns(1)]

        def emit_ln(x_ap, x_key, tag, par=0, n_cols=D):
            L = LNSx[par]
            return emit_layernorm(
                p, ar, x_ap, x_key, n_cols, lnscr[par], "lnscr",
                [L["s1"], L["s2"], L["t1"], L["t2"], L["yi"],
                 L["ii"], L["rstd"], L["nbias"]], f"LN{par}_")

        p.op("vector", lambda e: e.memset(inv128_bf, 1.0 / 128), [], ["inv128"])
        p.op("vector", lambda e: e.memset(ones_bf, 1.0), [], ["ones"])

        markP = ar.mark()
        qlT = ar.alloc(128, 512, F32)
        klT = ar.alloc(128, 512, F32)
        mark0 = ar.mark()

        wqkv_sb = ar.alloc(128, 8 * 3072, BF16)
        bias_rep = ar.alloc(128, 3072, F32)
        x0t = [ar.alloc(128, 1024, F32) for _ in range(3)]
        ln1b = [ar.alloc(128, 1024, BF16) for _ in range(2)]
        ln1T = [ar.alloc(128, 1024, BF16) for _ in range(3)]
        qkvb = [ar.alloc(128, 3072, BF16) for _ in range(2)]

        ps_qkv = [ps.alloc(128, 512) for _ in range(4)]
        ps_qlT = ps.alloc(128, 512)
        ps_klT = ps.alloc(128, 512)

        p.dma("gpsimd", wqkv_sb.rearrange("p (c n) -> p c n", c=8),
              wqkv_d.ap().rearrange("(c p) n -> p c n", p=128),
              [], ["wqkv"])
        p.dma("sync", bias_rep, bqkv_d[0:1, :].to_broadcast([128, 3072]),
              [], ["biasr"])

        def A_s0(i):
            p.dma("scalar", x0t[i % 3], x0_d[i * 128:(i + 1) * 128, :],
                  [], [f"x0_{i % 3}"])

        def A_s1(i):
            bi = i % 2
            x0, xk = x0t[i % 3], f"x0_{i % 3}"
            rstd_k, nbias_k = emit_ln(x0, xk, "", par=bi)
            lb, lbk = ln1b[bi], f"ln1b_{bi}"
            p.op("vector", lambda e, x0=x0, lb=lb, bi=bi: e.tensor_scalar(
                out=lb, in0=x0, scalar1=LNSx[bi]["rstd"],
                scalar2=LNSx[bi]["nbias"],
                op0=ALU.mult, op1=ALU.add),
                [xk, rstd_k, nbias_k], [lbk])
            p.dmatb(ln1T[i % 3], lb, 8, [lbk], [f"ln1T_{i % 3}"])

        def A_s2(i):
            bi = i % 2
            lT, lTk = ln1T[i % 3], f"ln1T_{i % 3}"
            qb, qbk = qkvb[bi], f"qkvb_{bi}"
            for nch in range(6):
                pst = ps_qkv[nch % 4]
                psk = f"psqkv_{nch % 4}"
                for dc in range(8):
                    p.mm(pst, lT[:, dc * 128:(dc + 1) * 128],
                         wqkv_sb[:, dc * 3072 + nch * 512: dc * 3072 + nch * 512 + 512],
                         [lTk, "wqkv"], [psk],
                         start=(dc == 0), stop=(dc == 7))
                p.op("vector", lambda e, pst=pst, qb=qb, nch=nch: e.tensor_add(
                    qb[:, nch * 512:(nch + 1) * 512], pst,
                    bias_rep[:, nch * 512:(nch + 1) * 512]),
                    [psk, "biasr"], [qbk])
            for dc in range(8):
                p.mm(ps_qlT[:, dc * 64 + i: dc * 64 + i + 1],
                     qb[:, dc * 128:(dc + 1) * 128], inv128_bf,
                     [qbk, "inv128"], ["psqlT"])
                p.mm(ps_klT[:, dc * 64 + i: dc * 64 + i + 1],
                     qb[:, 1024 + dc * 128: 1024 + (dc + 1) * 128], inv128_bf,
                     [qbk, "inv128"], ["psklT"])
            p.dma("gpsimd", qkv_d[i * 128:(i + 1) * 128, :], qb, [qbk], [])

        for i in range(NT + 3):
            if 3 <= i:
                A_s2(i - 3)
            if 1 <= i and i - 1 < NT:
                A_s1(i - 1)
            if i < NT:
                A_s0(i)

        p.op("vector", lambda e: e.tensor_copy(out=qlT, in_=ps_qlT),
             ["psqlT"], ["qlT"])
        p.op("vector", lambda e: e.tensor_copy(out=klT, in_=ps_klT),
             ["psklT"], ["klT"])
        if debug:
            p.dma("sync", dbg["qlT"][:, :], qlT, ["qlT"], [])
            p.dma("sync", dbg["klT"][:, :], klT, ["klT"], [])

        ar.release(mark0)
        ps.release(0)
        p.barrier()

        eye_sb = ar.alloc(128, 128, F32)
        p.dma("sync", eye_sb, eye128_d[:, :], [], ["eye"])
        qlbd_bf = ar.alloc(128, 1024, BF16)
        qlbd32 = ar.alloc(128, 1024, F32)
        klbd32 = ar.alloc(128, 1024, F32)
        for t_, t32, tbf, kk in ((qlT, qlbd32, qlbd_bf, "ql"),
                                 (klT, klbd32, klbd_bf, "kl")):
            p.op("vector", lambda e, t32=t32: e.memset(t32, 0.0),
                 [], [kk + "bd32"])
            for pp in range(8):
                p.op("vector", lambda e, t_=t_, t32=t32, pp=pp: e.tensor_copy(
                    out=t32[0:64, pp * 128:pp * 128 + 64],
                    in_=t_[0:64, pp * 64:(pp + 1) * 64]),
                    [kk + "T"], [kk + "bd32"])
                p.op("vector", lambda e, t_=t_, t32=t32, pp=pp: e.tensor_copy(
                    out=t32[64:128, pp * 128 + 64:(pp + 1) * 128],
                    in_=t_[64:128, pp * 64:(pp + 1) * 64]),
                    [kk + "T"], [kk + "bd32"])
            p.op("vector", lambda e, t32=t32, tbf=tbf: e.tensor_copy(
                out=tbf, in_=t32), [kk + "bd32"], [kk + "bd"])

        kT_sb = [ar.alloc(128, 8 * 512, BF16) for _ in range(2)]
        vext = [ar.alloc(128, 16 * 65, BF16) for _ in range(2)]
        p3t = [ar.alloc(128, 1024, BF16) for _ in range(2)]
        x2bd = ar.alloc(128, 1024, F32)
        x2Tbd = ar.alloc(128, 1024, F32)
        Vbd = [ar.alloc(128, 1024, F32) for _ in range(2)]
        VTbd = [ar.alloc(128, 1024, F32) for _ in range(2)]
        kvbd = ar.alloc(128, 1024, F32)
        kvtbd = ar.alloc(128, 1024, F32)
        bbd = ar.alloc(128, 1024, F32)
        mbd = ar.alloc(128, 1024, F32)
        VT_bf = ar.alloc(128, 1024, BF16)
        t3vn = ar.alloc(128, 512, BF16)
        t3vraw = ar.alloc(128, 1024, F32)
        sden = ar.alloc(128, 16, F32)

        ps_s3 = ps.alloc(128, 1024)
        ps_t3vE = ps.alloc(64, 1024)
        ps_t3vO = ps.alloc(128, 1024)
        ps_pv = [ps.alloc(128, 512) for _ in range(2)]

        def t3c(pp):
            return (pp // 7) * 512 + (pp % 7) * 65

        for b_ in range(2):
            p.op("vector", lambda e, b_=b_: e.memset(
                vext[b_].rearrange("p (h c) -> p h c", h=16)[:, :, 64:65], 1.0),
                [], [f"vext_{b_}"])

        def pinv_stage0():
            p.op("vector", lambda e: e.memset(x2bd, 0.0), [], ["x2bd"])
            for pp in range(8):
                pv, pvk = ps_pv[pp % 2], f"pspv{pp % 2}"
                sl = slice(pp * 128, (pp + 1) * 128)
                p.mm(pv[:, 0:128], qlbd32[:, sl], klbd32[:, sl],
                     ["qlbd32", "klbd32"], [pvk])
                p.op("scalar", lambda e, pv=pv, pp=pp: e.activation(
                    out=x2bd[0:64, pp * 128:pp * 128 + 64],
                    in_=pv[0:64, 0:64], func=AF.Exp, scale=SCALE),
                    [pvk], ["x2bd"])
                p.op("scalar", lambda e, pv=pv, pp=pp: e.activation(
                    out=x2bd[64:128, pp * 128 + 64:(pp + 1) * 128],
                    in_=pv[64:128, 64:128], func=AF.Exp, scale=SCALE),
                    [pvk], ["x2bd"])
                p.op("vector", lambda e, pp=pp: e.reduce_sum(
                    sden[0:64, 0:1], x2bd[0:64, pp * 128:pp * 128 + 64],
                    axis=AX.X), ["x2bd"], ["sd0"])
                p.op("vector", lambda e, pp=pp: e.reduce_sum(
                    sden[64:128, 0:1],
                    x2bd[64:128, pp * 128 + 64:(pp + 1) * 128],
                    axis=AX.X), ["x2bd"], ["sd0"])
                p.op("vector", lambda e: e.reciprocal(
                    out=sden[:, 1:2], in_=sden[:, 0:1]), ["sd0"], ["sd1"])
                p.op("vector", lambda e, pp=pp: e.tensor_scalar(
                    out=x2bd[0:64, pp * 128:pp * 128 + 64],
                    in0=x2bd[0:64, pp * 128:pp * 128 + 64],
                    scalar1=sden[0:64, 1:2], scalar2=None, op0=ALU.mult),
                    ["x2bd", "sd1"], ["x2bd"])
                p.op("vector", lambda e, pp=pp: e.tensor_scalar(
                    out=x2bd[64:128, pp * 128 + 64:(pp + 1) * 128],
                    in0=x2bd[64:128, pp * 128 + 64:(pp + 1) * 128],
                    scalar1=sden[64:128, 1:2], scalar2=None, op0=ALU.mult),
                    ["x2bd", "sd1"], ["x2bd"])
                p.op("tensor", lambda e, pv=pv, sl=sl: e.transpose(
                    pv[:, 0:128], x2bd[:, sl], eye_sb),
                    ["x2bd", "eye"], [pvk])
                p.op("vector", lambda e, pv=pv, sl=sl: e.tensor_copy(
                    out=x2Tbd[:, sl], in_=pv[:, 0:128]), [pvk], ["x2Tbd"])

        def pinv_iter(it):
            Vin = x2bd if it == 0 else Vbd[(it - 1) % 2]
            VTin = x2Tbd if it == 0 else VTbd[(it - 1) % 2]
            vik = "x2bd" if it == 0 else f"V{(it - 1) % 2}"
            vtik = "x2Tbd" if it == 0 else f"VT{(it - 1) % 2}"
            Vo, VTo = Vbd[it % 2], VTbd[it % 2]
            for pp in range(8):
                sl = slice(pp * 128, (pp + 1) * 128)
                pv0, pk0 = ps_pv[0], "pspv0"
                pv1, pk1 = ps_pv[1], "pspv1"
                p.mm(pv0[:, 0:128], x2Tbd[:, sl], Vin[:, sl],
                     ["x2Tbd", vik], [pk0])
                p.op("vector", lambda e, pv0=pv0, sl=sl: e.tensor_copy(
                    out=kvbd[:, sl], in_=pv0[:, 0:128]), [pk0], [f"kv{pp}"])
                p.mm(pv1[:, 0:128], Vin[:, sl], x2Tbd[:, sl],
                     [vik, "x2Tbd"], [pk1])
                p.op("vector", lambda e, pv1=pv1, sl=sl: e.tensor_copy(
                    out=kvtbd[:, sl], in_=pv1[:, 0:128]), [pk1], [f"kvt{pp}"])
                p.op("vector", lambda e, sl=sl: e.scalar_tensor_tensor(
                    out=bbd[:, sl], in0=eye_sb, scalar=7.0, in1=kvbd[:, sl],
                    op0=ALU.mult, op1=ALU.subtract),
                    ["eye", f"kv{pp}"], [f"b{pp}"])
                p.mm(pv0[:, 0:128], kvtbd[:, sl], bbd[:, sl],
                     [f"kvt{pp}", f"b{pp}"], [pk0])
                p.op("vector", lambda e, pv0=pv0, sl=sl: e.tensor_copy(
                    out=mbd[:, sl], in_=pv0[:, 0:128]), [pk0], [f"m{pp}"])
                p.op("vector", lambda e, sl=sl: e.scalar_tensor_tensor(
                    out=bbd[:, sl], in0=eye_sb, scalar=15.0, in1=mbd[:, sl],
                    op0=ALU.mult, op1=ALU.subtract),
                    ["eye", f"m{pp}"], [f"b{pp}"])
                p.mm(pv1[:, 0:128], kvtbd[:, sl], bbd[:, sl],
                     [f"kvt{pp}", f"b{pp}"], [pk1])
                p.op("vector", lambda e, pv1=pv1, sl=sl: e.tensor_copy(
                    out=mbd[:, sl], in_=pv1[:, 0:128]), [pk1], [f"m{pp}"])
                p.op("vector", lambda e, sl=sl: e.scalar_tensor_tensor(
                    out=bbd[:, sl], in0=eye_sb, scalar=13.0, in1=mbd[:, sl],
                    op0=ALU.mult, op1=ALU.subtract),
                    ["eye", f"m{pp}"], [f"b{pp}"])
                p.mm(pv0[:, 0:128], VTin[:, sl], bbd[:, sl],
                     [vtik, f"b{pp}"], [pk0])
                p.op("vector", lambda e, pv0=pv0, Vo=Vo, sl=sl: e.tensor_scalar(
                    out=Vo[:, sl], in0=pv0[:, 0:128], scalar1=0.25,
                    scalar2=None, op0=ALU.mult), [pk0], [f"V{it % 2}"])
                p.mm(pv1[:, 0:128], bbd[:, sl], VTin[:, sl],
                     [f"b{pp}", vtik], [pk1])
                p.op("vector", lambda e, pv1=pv1, VTo=VTo, sl=sl: e.tensor_scalar(
                    out=VTo[:, sl], in0=pv1[:, 0:128], scalar1=0.25,
                    scalar2=None, op0=ALU.mult), [pk1], [f"VT{it % 2}"])

        if stages >= 2:
            pinv_stage0()
        def B_loadk(g):
            if g >= (N_TOK // 512):
                return
            kt_, ktk_ = kT_sb[g % 2], f"kT_{g % 2}"
            for dc in range(8):
                q_ = "sync" if dc % 2 == 0 else "scalar"
                p.op(q_, lambda e, kt_=kt_, dc=dc, g=g: e.dma_start(
                    out=kt_[:, dc * 512:(dc + 1) * 512],
                    in_=qkv_d[g * 512:(g + 1) * 512,
                              1024 + dc * 128:1024 + (dc + 1) * 128],
                    transpose=True), [], [ktk_], inc=16)

        B_loadk(0)
        for i in range(NT):
            g, jj = i // 4, i % 4
            kt, ktk = kT_sb[g % 2], f"kT_{g % 2}"
            if jj == 0:
                B_loadk(g + 1)
            vx, vxk = vext[i % 2], f"vext_{i % 2}"
            p.dma("scalar",
                  vx.rearrange("p (h c) -> p h c", h=16)[:, :, 0:64],
                  qkv_d[i * 128:(i + 1) * 128, 2048:3072]
                  .rearrange("p (h c) -> p h c", h=16),
                  [], [vxk])
            for pp in range(8):
                p.mm(ps_s3[:, pp * 128:(pp + 1) * 128],
                     kt[:, pp * 512 + jj * 128:pp * 512 + (jj + 1) * 128],
                     qlbd_bf[:, pp * 128:(pp + 1) * 128],
                     [ktk, "qlbd"], ["ps_s3"])
            pb, pbk = p3t[i % 2], f"p3t_{i % 2}"
            p.op("scalar", lambda e, pb=pb: e.activation(
                out=pb, in_=ps_s3, func=AF.Exp, scale=SCALE),
                ["ps_s3"], [pbk])
            for pp in range(8):
                hA, hB = 2 * pp, 2 * pp + 1
                p.mm(ps_t3vE[:, t3c(pp):t3c(pp) + 65],
                     pb[:, hA * 64:(hA + 1) * 64], vx[:, hA * 65:(hA + 1) * 65],
                     [pbk, vxk], ["ps_t3vE"],
                     start=(i == 0 and pp in (0, 7)), stop=(i == NT - 1))
                p.op("tensor", lambda e, pp=pp, pb=pb, vx=vx, i=i: e.matmul(
                    ps_t3vO[64:128, t3c(pp):t3c(pp) + 65],
                    pb[:, (2 * pp + 1) * 64:(2 * pp + 2) * 64],
                    vx[:, (2 * pp + 1) * 65:(2 * pp + 2) * 65],
                    start=(i == 0 and pp in (0, 7)), stop=(i == NT - 1),
                    tile_position=(0, 64)),
                    [pbk, vxk], ["ps_t3vO"])
            if stages >= 3 and i % 8 == 7 and i // 8 < 6:
                pinv_iter(i // 8)

        if stages >= 4:
            p.op("vector", lambda e: e.tensor_copy(
                out=t3vraw[0:64, :], in_=ps_t3vE), ["ps_t3vE"], ["t3vraw"])
            p.op("vector", lambda e: e.tensor_copy(
                out=t3vraw[64:128, :], in_=ps_t3vO[64:128, :]),
                ["ps_t3vO"], ["t3vraw"])
            p.op("vector", lambda e: e.tensor_copy(out=VT_bf, in_=VTbd[1]),
                 ["VT1"], ["VT_bf"])
            p.op("vector", lambda e: e.memset(inner_bd, 0.0), [], ["inner_bd"])
            p.op("vector", lambda e: e.memset(
                inner_bd.rearrange("p (q c) -> p q c", q=8)[0:64, :, 64:65],
                1.0), [], ["inner_bd"])
            p.op("vector", lambda e: e.memset(
                inner_bd.rearrange("p (q c) -> p q c", q=8)[64:128, :, 129:130],
                1.0), [], ["inner_bd"])
            for pp in range(8):
                c = t3c(pp)
                p.op("vector", lambda e, c=c, pp=pp: e.reciprocal(
                    out=sden[:, 2 + pp:3 + pp], in_=t3vraw[:, c + 64:c + 65]),
                    ["t3vraw"], [f"rd{pp}"])
                p.op("vector", lambda e, c=c, pp=pp: e.tensor_scalar(
                    out=t3vn[:, pp * 64:(pp + 1) * 64],
                    in0=t3vraw[:, c:c + 64],
                    scalar1=sden[:, 2 + pp:3 + pp], scalar2=None, op0=ALU.mult),
                    ["t3vraw", f"rd{pp}"], ["t3vn"])
                pv, pvk = ps_pv[pp % 2], f"pspv{pp % 2}"
                p.mm(pv[:, 0:64], VT_bf[:, pp * 128:(pp + 1) * 128],
                     t3vn[:, pp * 64:(pp + 1) * 64],
                     ["VT_bf", "t3vn"], [pvk])
                p.op("vector", lambda e, pv=pv, pp=pp: e.tensor_copy(
                    out=inner_bd[0:64, pp * 130:pp * 130 + 64],
                    in_=pv[0:64, 0:64]), [pvk], ["inner_bd"])
                p.op("vector", lambda e, pv=pv, pp=pp: e.tensor_copy(
                    out=inner_bd[64:128, pp * 130 + 65:pp * 130 + 129],
                    in_=pv[64:128, 0:64]), [pvk], ["inner_bd"])
            if debug:
                p.dma("sync", dbg["attn2"][:, :], x2bd, ["x2bd"], [])
                p.dma("sync", dbg["pinvT"][:, :], VTbd[1], ["VT1"], [])
                p.dma("sync", dbg["t3v"][:, :], t3vraw, ["t3vraw"], [])
                p.dma("sync", dbg["inner"][:, :], inner_bd, ["inner_bd"], [])

        ar.release(markP)
        mark0 = markP
        ps.release(0)
        p.barrier()

        wout_sb = ar.alloc(128, 8 * 1024, BF16)
        bout_rep = ar.alloc(128, 1024, F32)
        p.dma("gpsimd", wout_sb.rearrange("p (c n) -> p c n", c=8),
              wout_d.ap().rearrange("(c p) n -> p c n", p=128),
              [], ["wout"])
        p.dma("sync", bout_rep, bout_d[0:1, :].to_broadcast([128, 1024]),
              [], ["boutr"])
        qT_sb = [ar.alloc(128, 8 * 512, BF16) for _ in range(2)]
        x0t = [ar.alloc(128, 1024, F32) for _ in range(3)]
        p1t = [ar.alloc(128, 1024, BF16) for _ in range(2)]
        attnb = [ar.alloc(128, 1024, BF16) for _ in range(2)]
        attnT = [ar.alloc(128, 1024, BF16) for _ in range(2)]
        attno = ar.alloc(128, 1024, F32)
        ln2f = ar.alloc(128, 1024, F32)
        x1t = [ar.alloc(128, 1024, F32) for _ in range(2)]
        dsc = ar.alloc(128, 4, F32)

        ps_s1 = ps.alloc(128, 1024)
        ps_oh = ps.alloc(128, 1536)
        ps_wo = [ps.alloc(128, 512) for _ in range(2)]

        def ohc(pp):
            return (pp // 3) * 512 + (pp % 3) * 130

        def C1_loadq(g):
            if g >= (N_TOK // 512):
                return
            qt, qtk = qT_sb[g % 2], f"qT_{g % 2}"
            for dc in range(8):
                q_ = "sync" if dc % 2 == 0 else "scalar"
                p.op(q_, lambda e, qt=qt, dc=dc, g=g: e.dma_start(
                    out=qt[:, dc * 512:(dc + 1) * 512],
                    in_=qkv_d[g * 512:(g + 1) * 512,
                              dc * 128:(dc + 1) * 128],
                    transpose=True), [], [qtk], inc=16)

        def C1_s0(i):
            if i % 4 == 0:
                C1_loadq(i // 4 + 1)
            p.dma("scalar", x0t[i % 3], x0_d[i * 128:(i + 1) * 128, :],
                  [], [f"x0c_{i % 3}"])

        def C1_s1(i):
            g, jj = i // 4, i % 4
            qt, qtk = qT_sb[g % 2], f"qT_{g % 2}"
            for pp in range(8):
                p.mm(ps_s1[:, pp * 128:(pp + 1) * 128],
                     klbd_bf[:, pp * 128:(pp + 1) * 128],
                     qt[:, pp * 512 + jj * 128:pp * 512 + (jj + 1) * 128],
                     ["klbd", qtk], ["ps_s1"])
            pb, pbk = p1t[i % 2], f"p1t_{i % 2}"
            p.op("scalar", lambda e, pb=pb: e.activation(
                out=pb, in_=ps_s1, func=AF.Exp, scale=SCALE),
                ["ps_s1"], [pbk])
            for pp in range(8):
                p.mm(ps_oh[:, ohc(pp):ohc(pp) + 130],
                     pb[:, pp * 128:(pp + 1) * 128],
                     inner_bd[:, pp * 130:(pp + 1) * 130],
                     [pbk, "inner_bd"], ["ps_oh"])
            ab, abk = attnb[i % 2], f"attnb_{i % 2}"
            for pp in range(8):
                c = ohc(pp)
                p.op("vector", lambda e, c=c: e.tensor_copy(
                    out=dsc[:, 0:2].unsqueeze(2),
                    in_=ps_oh[:, c:c + 130]
                    .rearrange("p (s c) -> p s c", s=2)[:, :, 64:65]),
                    ["ps_oh"], ["dsc"])
                p.op("vector", lambda e: e.reciprocal(
                    out=dsc[:, 2:4], in_=dsc[:, 0:2]), ["dsc"], ["dsc2"])
                p.op("vector", lambda e, c=c, ab=ab, pp=pp: e.tensor_mul(
                    ab[:, pp * 128:(pp + 1) * 128]
                    .rearrange("p (s c) -> p s c", s=2),
                    ps_oh[:, c:c + 130]
                    .rearrange("p (s c) -> p s c", s=2)[:, :, 0:64],
                    dsc[:, 2:4].unsqueeze(2).to_broadcast([128, 2, 64])),
                    ["ps_oh", "dsc2"], [abk + str(pp // 4)])
            at, atk = attnT[i % 2], f"attnT_{i % 2}"
            p.dmatb(at[:, 0:512], ab[:, 0:512], 4, [abk + "0"], [atk + "a"])
            p.dmatb(at[:, 512:1024], ab[:, 512:1024], 4,
                    [abk + "1"], [atk + "b"])

        def C1_s2(i):
            at, atk = attnT[i % 2], f"attnT_{i % 2}"
            for nch in range(2):
                pw, pwk = ps_wo[nch], f"pswo_{nch}"
                for kc in range(8):
                    p.mm(pw, at[:, kc * 128:(kc + 1) * 128],
                         wout_sb[:, kc * 1024 + nch * 512:kc * 1024 + nch * 512 + 512],
                         [atk + ("a" if kc < 4 else "b"), "wout"], [pwk],
                         start=(kc == 0), stop=(kc == 7))
                p.op("vector", lambda e, pw=pw, nch=nch: e.tensor_add(
                    attno[:, nch * 512:(nch + 1) * 512], pw,
                    bout_rep[:, nch * 512:(nch + 1) * 512]),
                    [pwk, "boutr"], ["attno"])
            rk, nk_ = emit_ln(attno, "attno", "", par=i % 2)
            p.op("vector", lambda e, j=i % 2: e.tensor_scalar(
                out=ln2f, in0=attno, scalar1=LNSx[j]["rstd"],
                scalar2=LNSx[j]["nbias"],
                op0=ALU.mult, op1=ALU.add), ["attno", rk, nk_], ["ln2f"])
            x1, x1k = x1t[i % 2], f"x1_{i % 2}"
            p.op("vector", lambda e, x1=x1, i=i: e.tensor_add(
                x1, ln2f, x0t[i % 3]),
                ["ln2f", f"x0c_{i % 3}"], [x1k])
            p.dma("gpsimd", x1_d[i * 128:(i + 1) * 128, :], x1, [x1k], [])

        C1_loadq(0)
        for i in range(NT + 2):
            if 1 <= i and i - 1 < NT:
                C1_s1(i - 1)
            if 2 <= i:
                C1_s2(i - 2)
            if i < NT:
                C1_s0(i)

        ar.release(mark0)
        ps.release(0)
        p.barrier()

        wup_sb = ar.alloc(128, 8 * 4096, BF16)
        wdn_sb = ar.alloc(128, 32 * 1024, BF16)
        wq_sb = ar.alloc(128, 8 * 512, BF16)
        keysbd_sb = ar.alloc(128, 512, F32)
        p.dma("gpsimd", wup_sb.rearrange("p (c n) -> p c n", c=8),
              wup_d.ap().rearrange("(c p) n -> p c n", p=128), [], ["wup"])
        p.dma("gpsimd", wdn_sb.rearrange("p (c n) -> p c n", c=32),
              wdown_d.ap().rearrange("(c p) n -> p c n", p=128), [], ["wdn"])
        p.dma("gpsimd", wq_sb.rearrange("p (c n) -> p c n", c=8),
              wq_d.ap().rearrange("(c p) n -> p c n", p=128), [], ["wq"])
        p.dma("sync", keysbd_sb, keysbd_d[:, :], [], ["keysbd"])

        x1t2 = [ar.alloc(128, 1024, F32) for _ in range(3)]
        ln3b = ar.alloc(128, 1024, BF16)
        ln3T = [ar.alloc(128, 1024, BF16) for _ in range(3)]
        u_bf = ar.alloc(128, 4096, BF16)
        uT = ar.alloc(128, 4096, BF16)
        th_sc = [ar.alloc(128, 512, BF16) for _ in range(2)]
        dense_sb = ar.alloc(128, 1024, F32)
        qfT_sb = ar.alloc(128, 512, F32)
        sim_sb = ar.alloc(128, 512, F32)
        mx8 = ar.alloc(128, 64, F32)
        msk = ar.alloc(128, 512, F32)
        ind_bf = ar.alloc(128, 512, BF16)
        eall = ar.alloc(128, 512, BF16)
        exm_sb = ar.alloc(128, 512, BF16)
        emx = ar.alloc(128, 64, BF16)
        candp = ar.alloc(128, 256, BF16)
        scp8 = ar.alloc(128, 32, BF16)
        zt = ar.alloc(128, 4, F32)
        selt = ar.alloc(128, 16, F32)

        ps_qf = ps.alloc(128, 512)
        ps_sim = ps.alloc(128, 512)
        ps_up = [ps.alloc(128, 512) for _ in range(4)]
        ps_dn = [ps.alloc(128, 512) for _ in range(2)]

        def C2_s0(i):
            p.dma("sync", x1t2[i % 3], x1_d[i * 128:(i + 1) * 128, :],
                  [], [f"x1c2_{i % 3}"])

        def C2_s1(i):
            x1, x1k = x1t2[i % 3], f"x1c2_{i % 3}"
            rk, nk_ = emit_ln(x1, x1k, "", par=i % 2)
            p.op("vector", lambda e, x1=x1, j=i % 2: e.tensor_scalar(
                out=ln3b, in0=x1, scalar1=LNSx[j]["rstd"],
                scalar2=LNSx[j]["nbias"],
                op0=ALU.mult, op1=ALU.add), [x1k, rk, nk_], ["ln3b"])
            lT, lTk = ln3T[i % 3], f"ln3T_{i % 3}"
            p.dmatb(lT, ln3b, 8, ["ln3b"], [lTk])

        def C2_s2(i):
            lT, lTk = ln3T[i % 3], f"ln3T_{i % 3}"
            for nch in range(8):
                pu, puk = ps_up[nch % 4], f"psup_{nch % 4}"
                for dc in range(8):
                    p.mm(pu, lT[:, dc * 128:(dc + 1) * 128],
                         wup_sb[:, dc * 4096 + nch * 512:dc * 4096 + nch * 512 + 512],
                         [lTk, "wup"], [puk],
                         start=(dc == 0), stop=(dc == 7))
                th = th_sc[nch % 2]
                thk = f"th_{nch % 2}"
                p.op("scalar", lambda e, pu=pu, th=th: e.activation(
                    out=th, in_=pu, func=AF.Tanh, scale=0.5),
                    [puk], [thk])
                p.op("vector", lambda e, th=th: e.tensor_scalar(
                    out=th, in0=th, scalar1=0.5, scalar2=0.5,
                    op0=ALU.mult, op1=ALU.add), [thk], [thk])
                p.op("vector", lambda e, pu=pu, nch=nch, th=th: e.tensor_mul(
                    u_bf[:, nch * 512:(nch + 1) * 512], th, pu),
                    [thk, puk], [f"u_{nch // 4}"])
            p.dmatb(uT[:, 0:2048], u_bf[:, 0:2048], 16, ["u_0"], ["uTa"])
            for c in range(4):
                for dc in range(8):
                    p.mm(ps_qf[:, c * 128:(c + 1) * 128],
                         wq_sb[:, dc * 512 + c * 128:dc * 512 + (c + 1) * 128],
                         lT[:, dc * 128:(dc + 1) * 128],
                         ["wq", lTk], ["ps_qf"],
                         start=(dc == 0), stop=(dc == 7))
            p.dmatb(uT[:, 2048:4096], u_bf[:, 2048:4096], 16,
                    ["u_1"], ["uTb"])
            p.op("scalar", lambda e: e.copy(out=qfT_sb, in_=ps_qf),
                 ["ps_qf"], ["qfT"])
            for c in range(4):
                p.mm(ps_sim[:, c * 128:(c + 1) * 128],
                     qfT_sb[:, c * 128:(c + 1) * 128],
                     keysbd_sb[:, c * 128:(c + 1) * 128],
                     ["qfT", "keysbd"], ["ps_sim"])
            p.op("scalar", lambda e: e.copy(out=sim_sb, in_=ps_sim),
                 ["ps_sim"], ["sim"])
            for nch in range(2):
                pd, pdk = ps_dn[nch], f"psdn_{nch}"
                for kc in range(32):
                    p.mm(pd, uT[:, kc * 128:(kc + 1) * 128],
                         wdn_sb[:, kc * 1024 + nch * 512:kc * 1024 + nch * 512 + 512],
                         ["uTa" if kc < 16 else "uTb", "wdn"], [pdk],
                         start=(kc == 0), stop=(kc == 31))
                p.op("scalar", lambda e, pd=pd, nch=nch: e.copy(
                    out=dense_sb[:, nch * 512:(nch + 1) * 512], in_=pd),
                    [pdk], ["dense_sb"])
            p.dma("gpsimd", dense_d[i * 128:(i + 1) * 128, :], dense_sb,
                  ["dense_sb"], [])
        def C2_s3(i):
            for gI in range(8):
                p.op("vector", lambda e, gI=gI: e.max(
                    out=mx8[:, gI * 8:(gI + 1) * 8],
                    in_=sim_sb[:, gI * 64:(gI + 1) * 64]), ["sim"], ["mx8"])
            for gI in range(8):
                p.op("vector", lambda e, gI=gI: e.match_replace(
                    out=msk[:, gI * 64:(gI + 1) * 64],
                    in_to_replace=mx8[:, gI * 8:(gI + 1) * 8],
                    in_values=sim_sb[:, gI * 64:(gI + 1) * 64],
                    imm_value=-BIG), ["mx8", "sim"], ["msk"])
            p.op("vector", lambda e: e.tensor_scalar(
                out=ind_bf, in0=msk, scalar1=-BIG / 2, scalar2=None,
                op0=ALU.is_le), ["msk"], ["ind"])
            p.op("scalar", lambda e: e.activation(
                out=eall, in_=sim_sb, func=AF.Exp), ["sim"], ["eall"])
            p.op("vector", lambda e: e.tensor_mul(exm_sb, eall, ind_bf),
                 ["eall", "ind"], ["exm"])
            p.dma("gpsimd", exm_d[i * 128:(i + 1) * 128, :], exm_sb,
                  ["exm"], [])
            p.op("scalar", lambda e: e.activation(
                out=emx, in_=mx8, func=AF.Exp), ["mx8"], ["emx"])
            p.op("vector", lambda e: e.tensor_mul(
                candp.rearrange("p (h a b) -> p h a b", h=4, a=8),
                emx[:, 0:32].rearrange("p (h a) -> p h a", h=4)
                    .unsqueeze(3).to_broadcast([128, 4, 8, 8]),
                emx[:, 32:64].rearrange("p (h b) -> p h b", h=4)
                    .unsqueeze(2).to_broadcast([128, 4, 8, 8])),
                ["emx"], ["candp"])
            for h in range(4):
                p.op("vector", lambda e, h=h: e.max(
                    out=scp8[:, h * 8:(h + 1) * 8],
                    in_=candp[:, h * 64:(h + 1) * 64]), ["candp"], ["scp8"])
            p.op("vector", lambda e: e.tensor_copy(
                out=selt[:, 0:4].unsqueeze(2),
                in_=scp8.rearrange("p (h k) -> p h k", h=4)[:, :, 7:8]),
                ["scp8"], ["selt"])
            p.op("vector", lambda e: e.reduce_sum(
                zt.unsqueeze(2), scp8.rearrange("p (h k) -> p h k", h=4),
                axis=AX.X), ["scp8"], ["zt"])
            p.op("vector", lambda e: e.reciprocal(
                out=selt[:, 4:8], in_=zt), ["zt"], ["selt"])
            p.dma("gpsimd", sel_d[i * 128:(i + 1) * 128, :], selt,
                  ["selt"], [])

        for i in range(NT + 3):
            if 3 <= i:
                C2_s2(i - 3)
            if 1 <= i and i - 1 < NT:
                C2_s1(i - 1)
            if 3 <= i:
                C2_s3(i - 3)
            if i < NT:
                C2_s0(i)

        ar.release(mark0)
        ps.release(0)
        p.barrier()

        de_sb = ar.alloc(128, 8 * 4096, BF16)
        ue_sb = ar.alloc(128, 32 * 1024, BF16)
        p.dma("gpsimd", de_sb.rearrange("p (c n) -> p c n", c=8),
              de_d.ap().rearrange("(c p) n -> p c n", p=128), [], ["de"])
        p.dma("gpsimd", ue_sb.rearrange("p (c n) -> p c n", c=32),
              ue_d.ap().rearrange("(c p) n -> p c n", p=128), [], ["ue"])

        x1f = [ar.alloc(128, 1024, F32) for _ in range(2)]
        densef = ar.alloc(128, 1024, F32)
        x1b = ar.alloc(128, 1024, F32)
        exmf = [ar.alloc(128, 512, BF16) for _ in range(2)]
        self_t = [ar.alloc(128, 16, F32) for _ in range(2)]
        ln3b2 = ar.alloc(128, 1024, BF16)
        ln3T2 = [ar.alloc(128, 1024, BF16) for _ in range(2)]
        exr = [ar.alloc(128, 256, BF16) for _ in range(2)]
        taur = [ar.alloc(128, 4, F32) for _ in range(2)]
        dots_bf = [ar.alloc(128, 4096, BF16) for _ in range(3)]
        G_bf = ar.alloc(128, 4096, BF16)
        md_bf = ar.alloc(128, 4096, BF16)
        wacc = ar.alloc(128, 4096, BF16)


        ps_dt = [ps.alloc(128, 512) for _ in range(4)]
        ps_x = ps.alloc(128, 1024)

        def F_s0(i):
            bi = i % 2
            p.dma("sync", x1f[bi], x1_d[i * 128:(i + 1) * 128, :],
                  [], [f"x1f_{bi}"])
            p.dma("sync", exmf[bi], exm_d[i * 128:(i + 1) * 128, :],
                  [], [f"exmf_{bi}"])
            p.dma("sync", self_t[bi], sel_d[i * 128:(i + 1) * 128, :],
                  [], [f"self_{bi}"])

        def F_s1(i):
            bi = i % 2
            ti = i % 3
            x1, x1k = x1f[bi], f"x1f_{bi}"
            ex, exk = exmf[bi], f"exmf_{bi}"
            se, sek = self_t[bi], f"self_{bi}"
            rk, nk_ = emit_ln(x1, x1k, "", par=bi)
            p.op("vector", lambda e, x1=x1, bi=bi: e.tensor_scalar(
                out=ln3b2, in0=x1, scalar1=LNSx[bi]["rstd"],
                scalar2=LNSx[bi]["nbias"],
                op0=ALU.mult, op1=ALU.add), [x1k, rk, nk_], ["ln3b2"])
            lT, lTk = ln3T2[bi], f"ln3T2_{bi}"
            p.dmatb(lT, ln3b2, 8, ["ln3b2"], [lTk])
            p.op("vector", lambda e, se=se, bi=bi: e.tensor_mul(
                taur[bi], se[:, 0:4], se[:, 4:8]), [sek], [f"taur_{bi}"])
            for h in range(FH):
                p.op("vector", lambda e, ex=ex, se=se, h=h, bi=bi: e.tensor_scalar(
                    out=exr[bi][:, h * 64:(h + 1) * 64],
                    in0=ex[:, h * 64:(h + 1) * 64],
                    scalar1=se[:, 4 + h:5 + h], scalar2=None, op0=ALU.mult),
                    [exk, sek], [f"exr_{bi}"])
            dot = dots_bf[ti]
            for ec in range(8):
                pd, pdk = ps_dt[ec % 4], f"psdt_{ec % 4}"
                for dc in range(8):
                    p.mm(pd, lT[:, dc * 128:(dc + 1) * 128],
                         de_sb[:, dc * 4096 + ec * 512:dc * 4096 + ec * 512 + 512],
                         [lTk, "de"], [pdk],
                         start=(dc == 0), stop=(dc == 7))
                if ec % 4 == 3:
                    half = ec // 4
                    p.op("scalar", lambda e, half=half, dot=dot: e.copy(
                        out=dot[:, half * 2048:(half + 1) * 2048],
                        in_=bass.AP(ps_dt[0].tensor, ps_dt[0].offset,
                                    [ps_dt[0].ap[0], [1, 2048]])),
                        [f"psdt_{k}" for k in range(4)],
                        [f"dots_{ti}{half}"])

        def F_s2(i):
            bi = i % 2
            ti = i % 3
            ex, exk = exmf[bi], f"exmf_{bi}"
            dot = dots_bf[ti]
            def unitA(u):
                h, q = u // 2, u % 2
                eyh = ex[:, 256 + h * 64:256 + (h + 1) * 64]
                rows = slice(q * 32, (q + 1) * 32)
                csl = slice(q * 2048, (q + 1) * 2048)
                gk, mk = f"G{q}", f"md{q}"
                Gv, mdv = G_bf[:, csl], md_bf[:, csl]
                p.op("vector", lambda e, eyh=eyh, rows=rows, Gv=Gv, h=h:
                     e.tensor_mul(
                    Gv.rearrange("p (a b) -> p a b", a=32),
                    exr[bi][:, h * 64:(h + 1) * 64][:, rows]
                    .unsqueeze(2).to_broadcast([128, 32, 64]),
                    eyh.unsqueeze(1).to_broadcast([128, 32, 64])),
                    [f"exr_{bi}", exk], [gk])
                p.op("vector", lambda e, h=h, Gv=Gv, mdv=mdv: e.tensor_scalar(
                    out=mdv, in0=Gv, scalar1=taur[bi][:, h:h + 1],
                    scalar2=None, op0=ALU.is_ge),
                    [gk, f"taur_{bi}"], [mk])
                p.op("vector", lambda e, mdv=mdv, csl=csl: e.tensor_mul(
                    mdv, mdv, dot[:, csl]), [mk, f"dots_{ti}{q}"], [mk])
                p.op("vector", lambda e, Gv=Gv, mdv=mdv: e.tensor_mul(
                    mdv, Gv, mdv), [gk, mk], [mk])
                p.op("scalar", lambda e, Gv=Gv, mdv=mdv: e.activation(
                    out=Gv, in_=mdv, func=AF.Tanh, scale=0.5), [mk], [gk])
                p.op("scalar", lambda e, Gv=Gv: e.activation(
                    out=Gv, in_=Gv, func=AF.Copy, scale=0.5, bias=0.5),
                    [gk], [gk])

            def unitB(u):
                h, q = u // 2, u % 2
                csl = slice(q * 2048, (q + 1) * 2048)
                gk, mk = f"G{q}", f"md{q}"
                Gv, mdv = G_bf[:, csl], md_bf[:, csl]
                p.op("vector", lambda e, Gv=Gv, mdv=mdv: e.tensor_mul(
                    mdv, Gv, mdv), [gk, mk], [mk])
                if h == 0:
                    p.op("vector", lambda e, mdv=mdv, csl=csl: e.tensor_copy(
                        out=wacc[:, csl], in_=mdv), [mk], [f"wacc{q}"])
                else:
                    p.op("vector", lambda e, mdv=mdv, csl=csl: e.tensor_add(
                        wacc[:, csl], wacc[:, csl], mdv),
                        [mk, f"wacc{q}"], [f"wacc{q}"])

            for u in range(9):
                if u < 8:
                    unitA(u)
                if u >= 1:
                    unitB(u - 1)
            p.dmatb(dot[:, 0:2048], wacc[:, 0:2048], 16,
                    ["wacc0"], [f"dots_{ti}0"])
            p.dmatb(dot[:, 2048:4096], wacc[:, 2048:4096], 16,
                    ["wacc1"], [f"dots_{ti}1"])

        def F_s3(i):
            bi = i % 2
            ti = i % 3
            dot = dots_bf[ti]
            dotk = [f"dots_{ti}0", f"dots_{ti}1"]
            p.dma("sync", densef, dense_d[i * 128:(i + 1) * 128, :],
                  [], ["densef"])
            p.dma("sync", x1b, x1_d[i * 128:(i + 1) * 128, :], [], ["x1b"])
            for nch in range(2):
                for kc in range(32):
                    p.mm(ps_x[:, nch * 512:(nch + 1) * 512],
                         dot[:, kc * 128:(kc + 1) * 128],
                         ue_sb[:, kc * 1024 + nch * 512:kc * 1024 + nch * 512 + 512],
                         [dotk[kc // 16], "ue"], ["ps_x"],
                         start=(kc == 0), stop=(kc == 31))
            p.op("vector", lambda e: e.tensor_add(densef, ps_x, densef),
                 ["ps_x", "densef"], ["densef"])
            rk2, nk2 = emit_ln(densef, "densef", "", par=bi)
            p.op("vector", lambda e, bi=bi: e.tensor_scalar(
                out=densef, in0=densef, scalar1=LNSx[bi]["rstd"],
                scalar2=LNSx[bi]["nbias"],
                op0=ALU.mult, op1=ALU.add), ["densef", rk2, nk2], ["densef"])
            p.op("vector", lambda e: e.tensor_add(densef, densef, x1b),
                 ["densef", "x1b"], ["densef"])
            p.dma("gpsimd", out_d[i * 128:(i + 1) * 128, :], densef,
                  ["densef"], [])

        for i in range(NT + 3):
            if 2 <= i and i - 2 < NT:
                F_s2(i - 2)
            if 3 <= i:
                F_s3(i - 3)
            if 1 <= i and i - 1 < NT:
                F_s1(i - 1)
            if i < NT:
                F_s0(i)

        ar.release(mark0)
        ps.release(0)
        p.barrier()

        with nc.Block() as block:
            p.emit(block, sems)
    return nc, dbg


def kernel(**inputs):
    shared, per_core = host_prepare(inputs)
    nc, _ = build(debug=False)
    in_maps = [dict(shared, **pc) for pc in per_core]
    res = run_bass_kernel_spmd(nc, in_maps, core_ids=list(range(N_CORES)))
    out = np.stack([np.asarray(r["out"]) for r in res.results], axis=0)
    return out.astype(np.float32)


# version 22
# speedup vs baseline: 1.0331x; 1.0331x over previous
import numpy as np

import concourse.bass as bass
import concourse.mybir as mybir
from concourse.bass_utils import run_bass_kernel_spmd

F32 = mybir.dt.float32
BF16 = mybir.dt.bfloat16
I32 = mybir.dt.int32
U8 = mybir.dt.uint8
AF = mybir.ActivationFunctionType
ALU = mybir.AluOpType
AX = mybir.AxisListType

N_CORES = 8
N_TOK = 8192
D = 1024
NT = N_TOK // 128
H = 16
DH = 64
M = 64
SCALE = 0.125
EPS = 1e-5
FH = 4
NK = 64
NEXP = 4096
I_FFN = 4096
BIG = 1e30
MAGIC = 0x5F3759DF

ENGINES = ("sync", "tensor", "vector", "scalar", "gpsimd")
DTSIZE = {F32: 4, BF16: 2, I32: 4, U8: 1}

ARENA_BYTES = 212700
PSUM_BANKS = 8


DMA_POOL = 24


class Prog:

    def __init__(self, nc):
        self.nc = nc
        self.ops = []
        self.last_writer = {}
        self.readers = {}
        self.floor = set()
        self.dma_count = {}

    def op(self, eng, fn, reads=(), writes=(), inc=1):
        idx = len(self.ops)
        deps = set(self.floor)
        for b in reads:
            w = self.last_writer.get(b)
            if w is not None:
                deps.add(w)
        for b in writes:
            w = self.last_writer.get(b)
            if w is not None:
                deps.add(w)
            deps.update(self.readers.get(b, ()))
        if inc == 16:
            n = self.dma_count.get(eng, 0)
            self.dma_count[eng] = n + 1
            sem_id = f"{eng}#{n % DMA_POOL}"
        else:
            sem_id = eng
        self.ops.append([eng, fn, deps, inc, sem_id])
        for b in reads:
            self.readers.setdefault(b, []).append(idx)
        for b in writes:
            self.last_writer[b] = idx
            self.readers[b] = []
        return idx

    def dma(self, q, out, in_, reads, writes, **kw):
        self.op(q, lambda e: e.dma_start(out=out, in_=in_, **kw), reads, writes, inc=16)

    def dmat(self, out, in_, reads, writes):
        self.op("sync", lambda e: e.dma_start(out=out, in_=in_, transpose=True),
                reads, writes, inc=16)

    def dmatb(self, out, in_, chunks, reads, writes):
        self.op("sync", lambda e: e.dma_start(
            out=out.rearrange("p (c f) -> p c f", c=chunks),
            in_=in_.rearrange("p (c f) -> p c f", c=chunks),
            transpose=True), reads, writes, inc=16)

    def mm(self, out, lhsT, rhs, reads, writes, start=True, stop=True):
        self.op("tensor",
                lambda e: e.matmul(out, lhsT, rhs, start=start, stop=stop),
                reads, writes)

    def barrier(self):
        last_by_sem = {}
        for i, rec in enumerate(self.ops):
            last_by_sem[rec[4]] = i
        deps = set(last_by_sem.values()) | self.floor
        new_floor = set()
        for eng in ENGINES:
            idx = len(self.ops)
            self.ops.append([eng, lambda e: e.nop(), set(deps), 1, eng])
            new_floor.add(idx)
        self.floor = new_floor

    def emit(self, block, sems):
        tok = [None] * len(self.ops)
        counts = {}
        for i, (eng, _, _, inc, sem_id) in enumerate(self.ops):
            counts[sem_id] = counts.get(sem_id, 0) + inc
            tok[i] = (sem_id, counts[sem_id])
        per_eng = {e: [] for e in ENGINES}
        for i, rec in enumerate(self.ops):
            per_eng[rec[0]].append(rec)
        self.final_counts = counts

        for eng in ENGINES:
            lst = per_eng[eng]
            if not lst:
                continue

            def body(engine, lst=lst, eng=eng):
                waited = {}
                for eng_, fn, deps, inc, sem_id in lst:
                    need = {}
                    for j in deps:
                        s2, v = tok[j]
                        if s2 == "tensor" and eng == "tensor":
                            continue
                        if v > waited.get(s2, 0):
                            need[s2] = max(need.get(s2, 0), v)
                    for s2 in sorted(need):
                        engine.wait_ge(sems[s2], need[s2])
                        waited[s2] = need[s2]
                    inst = fn(engine)
                    if inst is not None:
                        inst.then_inc(sems[sem_id], inc)

            getattr(block, eng)(body)


class Arena:

    def __init__(self, t):
        self.t = t
        self.off = 0

    def alloc(self, parts, free, dtype, align=64):
        nbytes = free * DTSIZE[dtype]
        self.off = (self.off + align - 1) // align * align
        assert self.off + nbytes <= ARENA_BYTES, \
            f"SBUF arena overflow: {self.off}+{nbytes}"
        ap = self.t[0:parts, self.off:self.off + nbytes].bitcast(dtype)
        self.off += nbytes
        return ap

    def mark(self):
        return self.off

    def release(self, mark):
        self.off = mark


class PsumArena:

    def __init__(self, t):
        self.t = t
        self.bank = 0

    def alloc(self, parts, free_f32):
        banks = (free_f32 + 511) // 512
        assert self.bank + banks <= PSUM_BANKS, "PSUM overflow"
        ap = self.t[0:parts, self.bank * 512:self.bank * 512 + free_f32]
        self.bank += banks
        return ap

    def mark(self):
        return self.bank

    def release(self, mark):
        self.bank = mark



def _build_keysbd(keys):
    kt = np.asarray(keys, dtype=np.float32).transpose(2, 0, 3, 1)
    out = np.zeros((128, 512), dtype=np.float32)
    for pnum in range(2):
        for h in range(4):
            c = pnum * 2 + h // 2
            lo = (h % 2) * 64
            out[lo:lo + 64, c * 128 + lo:c * 128 + lo + 64] = kt[pnum, h]
    return np.ascontiguousarray(out)


def host_prepare(inputs):
    x = np.asarray(inputs["L0_attn_hidden_states"], dtype=np.float32)
    consts = np.zeros((128, 128), dtype=np.float32)
    np.fill_diagonal(consts, 1.0)
    shared = {
        "wqkv": np.ascontiguousarray(inputs["L0_attn_Wqkv"], dtype=np.float32),
        "bqkv": np.ascontiguousarray(
            np.asarray(inputs["L0_attn_bqkv"], dtype=np.float32).reshape(1, 3 * D)),
        "wout": np.ascontiguousarray(inputs["L0_attn_Wout"], dtype=np.float32),
        "bout": np.ascontiguousarray(
            np.asarray(inputs["L0_attn_bout"], dtype=np.float32).reshape(1, D)),
        "wup_t": np.ascontiguousarray(
            np.asarray(inputs["L0_ffn_W_up"], dtype=np.float32).T),
        "wdown_t": np.ascontiguousarray(
            np.asarray(inputs["L0_ffn_W_down"], dtype=np.float32).T),
        "wq_t": np.ascontiguousarray(
            np.asarray(inputs["L0_ffn_W_q"], dtype=np.float32).T),
        "keysbd": _build_keysbd(inputs["L0_ffn_keys"]),
        "de_t": np.ascontiguousarray(
            np.asarray(inputs["L0_ffn_down_embed"], dtype=np.float32).T),
        "ue": np.ascontiguousarray(inputs["L0_ffn_up_embed"], dtype=np.float32),
        "eye128": consts,
    }
    per_core = [{"x0": np.ascontiguousarray(x[c])} for c in range(N_CORES)]
    return shared, per_core



def emit_layernorm(p, ar, x_ap, x_key, n_cols, scratch, scratch_key,
                   outs, tag):
    s1, s2, t1, t2, yi, ii, rstd, nbias = outs
    k = tag
    p.op("vector", lambda e: e.reduce_sum(s1, x_ap, axis=AX.X),
         [x_key], [k + "s1"])
    p.op("scalar", lambda e: e.activation(out=scratch, in_=x_ap, func=AF.Square,
                                          accum_out=s2),
         [x_key], [scratch_key, k + "s2"])
    inv_n = 1.0 / n_cols
    p.op("vector", lambda e: e.tensor_scalar(out=t1, in0=s1, scalar1=inv_n,
                                             scalar2=None, op0=ALU.mult),
         [k + "s1"], [k + "t1"])
    p.op("vector", lambda e: e.tensor_scalar(out=t2, in0=s2, scalar1=inv_n,
                                             scalar2=None, op0=ALU.mult),
         [k + "s2"], [k + "t2"])
    p.op("vector", lambda e: e.tensor_mul(s1, t1, t1), [k + "t1"], [k + "s1"])
    p.op("vector", lambda e: e.tensor_sub(s2, t2, s1),
         [k + "t2", k + "s1"], [k + "s2"])
    p.op("vector", lambda e: e.tensor_scalar(out=s2, in0=s2, scalar1=EPS,
                                             scalar2=None, op0=ALU.add),
         [k + "s2"], [k + "s2"])
    p.op("vector", lambda e: e.tensor_scalar(out=ii, in0=s2.bitcast(I32),
                                             scalar1=1, scalar2=None,
                                             op0=ALU.logical_shift_right),
         [k + "s2"], [k + "ii"])
    p.op("vector", lambda e: e.tensor_scalar(out=ii, in0=ii, scalar1=-1,
                                             scalar2=MAGIC, op0=ALU.mult,
                                             op1=ALU.add),
         [k + "ii"], [k + "ii"])
    p.op("vector", lambda e: e.tensor_copy(out=rstd, in_=ii.bitcast(F32)),
         [k + "ii"], [k + "rstd"])
    p.op("vector", lambda e: e.tensor_scalar(out=yi, in0=s2, scalar1=-0.5,
                                             scalar2=None, op0=ALU.mult),
         [k + "s2"], [k + "yi"])
    for it in range(4):
        p.op("vector", lambda e: e.tensor_mul(t2, rstd, rstd),
             [k + "rstd"], [k + "t2"])
        p.op("vector", lambda e: e.tensor_scalar(out=t2, in0=t2, scalar1=yi,
                                                 scalar2=1.5, op0=ALU.mult,
                                                 op1=ALU.add),
             [k + "t2", k + "yi"], [k + "t2"])
        p.op("vector", lambda e: e.tensor_mul(rstd, rstd, t2),
             [k + "rstd", k + "t2"], [k + "rstd"])
    p.op("vector", lambda e: e.scalar_tensor_tensor(out=nbias, in0=t1,
                                                    scalar=-1.0, in1=rstd,
                                                    op0=ALU.mult, op1=ALU.mult),
         [k + "t1", k + "rstd"], [k + "nbias"])
    return k + "rstd", k + "nbias"



def build(debug=False, stages=99, micro=99):
    nc = bass.Bass("TRN2", target_bir_lowering=False)

    x0_d = nc.dram_tensor("x0", [N_TOK, D], F32, kind="ExternalInput")
    wqkv_d = nc.dram_tensor("wqkv", [D, 3 * D], F32, kind="ExternalInput")
    bqkv_d = nc.dram_tensor("bqkv", [1, 3 * D], F32, kind="ExternalInput")
    wout_d = nc.dram_tensor("wout", [D, D], F32, kind="ExternalInput")
    bout_d = nc.dram_tensor("bout", [1, D], F32, kind="ExternalInput")
    wup_d = nc.dram_tensor("wup_t", [D, I_FFN], F32, kind="ExternalInput")
    wdown_d = nc.dram_tensor("wdown_t", [I_FFN, D], F32, kind="ExternalInput")
    wq_d = nc.dram_tensor("wq_t", [D, 512], F32, kind="ExternalInput")
    keysbd_d = nc.dram_tensor("keysbd", [128, 512], F32, kind="ExternalInput")
    de_d = nc.dram_tensor("de_t", [D, NEXP], F32, kind="ExternalInput")
    ue_d = nc.dram_tensor("ue", [NEXP, D], F32, kind="ExternalInput")
    eye128_d = nc.dram_tensor("eye128", [128, 128], F32, kind="ExternalInput")

    out_d = nc.dram_tensor("out", [N_TOK, D], F32, kind="ExternalOutput")

    skind = dict(kind="ExternalOutput") if debug else {}
    qkv_d = nc.dram_tensor("qkv_s", [N_TOK, 3 * D], BF16, **skind)
    x1_d = nc.dram_tensor("x1_s", [N_TOK, D], F32, **skind)
    dense_d = nc.dram_tensor("dense_s", [N_TOK, D], F32, **skind)
    exm_d = nc.dram_tensor("exm_s", [N_TOK, 512], BF16, **skind)
    sel_d = nc.dram_tensor("sel_s", [N_TOK, 16], F32, **skind)

    dbg = {}

    def dbg_out(name, shape, dt=F32):
        if debug:
            dbg[name] = nc.dram_tensor("dbg_" + name, shape, dt,
                                       kind="ExternalOutput")
        return dbg.get(name)

    dbg_out("qlT", [128, 512])
    dbg_out("klT", [128, 512])
    dbg_out("attn2", [128, 1024])
    dbg_out("pinvT", [128, 1024])
    dbg_out("t3v", [128, 1024])
    dbg_out("inner", [128, 8 * 130], BF16)

    from contextlib import ExitStack
    with ExitStack() as stack:
        sems = {e: stack.enter_context(nc.semaphore(f"S_{e}")) for e in ENGINES}
        for q in ("sync", "scalar", "gpsimd"):
            for j in range(DMA_POOL):
                sems[f"{q}#{j}"] = stack.enter_context(
                    nc.semaphore(f"S_{q}_{j}"))
        arena_t = stack.enter_context(
            nc.sbuf_tensor("arena", [128, ARENA_BYTES], U8))
        psum_t = stack.enter_context(
            nc.psum_tensor("psumA", [128, 4096], F32))
        ar = Arena(arena_t)
        ps = PsumArena(psum_t)
        p = Prog(nc)

        klbd_bf = ar.alloc(128, 1024, BF16)
        inner_bd = ar.alloc(128, 8 * 130, BF16)
        inv128_bf = ar.alloc(128, 1, BF16)
        ones_bf = ar.alloc(128, 1, BF16)
        smalls = [ar.alloc(128, 16, F32) for _ in range(2)]
        smalls_i = [ar.alloc(128, 1, I32) for _ in range(2)]
        lnscr = [ar.alloc(128, 1024, BF16)] * 2

        def _lns(j):
            sm, si = smalls[j], smalls_i[j]
            return dict(
                s1=sm[:, 0:1], s2=sm[:, 1:2], t1=sm[:, 2:3],
                t2=sm[:, 3:4], yi=sm[:, 4:5], rstd=sm[:, 5:6],
                nbias=sm[:, 6:7], ii=si[:, 0:1],
            )
        LNSx = [_lns(0), _lns(1)]

        def emit_ln(x_ap, x_key, tag, par=0, n_cols=D):
            L = LNSx[par]
            return emit_layernorm(
                p, ar, x_ap, x_key, n_cols, lnscr[par], "lnscr",
                [L["s1"], L["s2"], L["t1"], L["t2"], L["yi"],
                 L["ii"], L["rstd"], L["nbias"]], f"LN{par}_")

        p.op("vector", lambda e: e.memset(inv128_bf, 1.0 / 128), [], ["inv128"])
        p.op("vector", lambda e: e.memset(ones_bf, 1.0), [], ["ones"])

        markP = ar.mark()
        qlT = ar.alloc(128, 512, F32)
        klT = ar.alloc(128, 512, F32)
        mark0 = ar.mark()

        wqkv_sb = ar.alloc(128, 8 * 3072, BF16)
        bias_rep = ar.alloc(128, 3072, F32)
        x0t = [ar.alloc(128, 1024, F32) for _ in range(3)]
        ln1b = [ar.alloc(128, 1024, BF16) for _ in range(2)]
        ln1T = [ar.alloc(128, 1024, BF16) for _ in range(3)]
        qkvb = [ar.alloc(128, 3072, BF16) for _ in range(2)]

        ps_qkv = [ps.alloc(128, 512) for _ in range(4)]
        ps_qlT = ps.alloc(128, 512)
        ps_klT = ps.alloc(128, 512)

        p.dma("gpsimd", wqkv_sb.rearrange("p (c n) -> p c n", c=8),
              wqkv_d.ap().rearrange("(c p) n -> p c n", p=128),
              [], ["wqkv"])
        p.dma("sync", bias_rep, bqkv_d[0:1, :].to_broadcast([128, 3072]),
              [], ["biasr"])

        def A_s0(i):
            p.dma("scalar", x0t[i % 3], x0_d[i * 128:(i + 1) * 128, :],
                  [], [f"x0_{i % 3}"])

        def A_s1(i):
            bi = i % 2
            x0, xk = x0t[i % 3], f"x0_{i % 3}"
            rstd_k, nbias_k = emit_ln(x0, xk, "", par=bi)
            lb, lbk = ln1b[bi], f"ln1b_{bi}"
            p.op("vector", lambda e, x0=x0, lb=lb, bi=bi: e.tensor_scalar(
                out=lb, in0=x0, scalar1=LNSx[bi]["rstd"],
                scalar2=LNSx[bi]["nbias"],
                op0=ALU.mult, op1=ALU.add),
                [xk, rstd_k, nbias_k], [lbk])
            p.dmatb(ln1T[i % 3], lb, 8, [lbk], [f"ln1T_{i % 3}"])

        def A_s2(i):
            bi = i % 2
            lT, lTk = ln1T[i % 3], f"ln1T_{i % 3}"
            qb, qbk = qkvb[bi], f"qkvb_{bi}"
            for nch in range(6):
                pst = ps_qkv[nch % 4]
                psk = f"psqkv_{nch % 4}"
                for dc in range(8):
                    p.mm(pst, lT[:, dc * 128:(dc + 1) * 128],
                         wqkv_sb[:, dc * 3072 + nch * 512: dc * 3072 + nch * 512 + 512],
                         [lTk, "wqkv"], [psk],
                         start=(dc == 0), stop=(dc == 7))
                p.op("vector", lambda e, pst=pst, qb=qb, nch=nch: e.tensor_add(
                    qb[:, nch * 512:(nch + 1) * 512], pst,
                    bias_rep[:, nch * 512:(nch + 1) * 512]),
                    [psk, "biasr"], [qbk])
            for dc in range(8):
                p.mm(ps_qlT[:, dc * 64 + i: dc * 64 + i + 1],
                     qb[:, dc * 128:(dc + 1) * 128], inv128_bf,
                     [qbk, "inv128"], ["psqlT"])
                p.mm(ps_klT[:, dc * 64 + i: dc * 64 + i + 1],
                     qb[:, 1024 + dc * 128: 1024 + (dc + 1) * 128], inv128_bf,
                     [qbk, "inv128"], ["psklT"])
            p.dma("gpsimd", qkv_d[i * 128:(i + 1) * 128, :], qb, [qbk], [])

        for i in range(NT + 3):
            if 3 <= i:
                A_s2(i - 3)
            if 1 <= i and i - 1 < NT:
                A_s1(i - 1)
            if i < NT:
                A_s0(i)

        p.op("vector", lambda e: e.tensor_copy(out=qlT, in_=ps_qlT),
             ["psqlT"], ["qlT"])
        p.op("vector", lambda e: e.tensor_copy(out=klT, in_=ps_klT),
             ["psklT"], ["klT"])
        if debug:
            p.dma("sync", dbg["qlT"][:, :], qlT, ["qlT"], [])
            p.dma("sync", dbg["klT"][:, :], klT, ["klT"], [])

        ar.release(mark0)
        ps.release(0)
        p.barrier()

        eye_sb = ar.alloc(128, 128, F32)
        p.dma("sync", eye_sb, eye128_d[:, :], [], ["eye"])
        qlbd_bf = ar.alloc(128, 1024, BF16)
        qlbd32 = ar.alloc(128, 1024, F32)
        klbd32 = ar.alloc(128, 1024, F32)
        for t_, t32, tbf, kk in ((qlT, qlbd32, qlbd_bf, "ql"),
                                 (klT, klbd32, klbd_bf, "kl")):
            p.op("vector", lambda e, t32=t32: e.memset(t32, 0.0),
                 [], [kk + "bd32"])
            for pp in range(8):
                p.op("vector", lambda e, t_=t_, t32=t32, pp=pp: e.tensor_copy(
                    out=t32[0:64, pp * 128:pp * 128 + 64],
                    in_=t_[0:64, pp * 64:(pp + 1) * 64]),
                    [kk + "T"], [kk + "bd32"])
                p.op("vector", lambda e, t_=t_, t32=t32, pp=pp: e.tensor_copy(
                    out=t32[64:128, pp * 128 + 64:(pp + 1) * 128],
                    in_=t_[64:128, pp * 64:(pp + 1) * 64]),
                    [kk + "T"], [kk + "bd32"])
            p.op("vector", lambda e, t32=t32, tbf=tbf: e.tensor_copy(
                out=tbf, in_=t32), [kk + "bd32"], [kk + "bd"])

        kT_sb = [ar.alloc(128, 8 * 512, BF16) for _ in range(2)]
        vext = [ar.alloc(128, 16 * 65, BF16) for _ in range(2)]
        p3t = [ar.alloc(128, 1024, BF16) for _ in range(2)]
        x2bd = ar.alloc(128, 1024, F32)
        x2Tbd = ar.alloc(128, 1024, F32)
        Vbd = [ar.alloc(128, 1024, F32) for _ in range(2)]
        VTbd = [ar.alloc(128, 1024, F32) for _ in range(2)]
        kvbd = ar.alloc(128, 1024, F32)
        kvtbd = ar.alloc(128, 1024, F32)
        bbd = ar.alloc(128, 1024, F32)
        mbd = ar.alloc(128, 1024, F32)
        VT_bf = ar.alloc(128, 1024, BF16)
        t3vn = ar.alloc(128, 512, BF16)
        t3vraw = ar.alloc(128, 1024, F32)
        sden = ar.alloc(128, 16, F32)

        ps_s3 = ps.alloc(128, 1024)
        ps_t3vE = ps.alloc(64, 1024)
        ps_t3vO = ps.alloc(128, 1024)
        ps_pv = [ps.alloc(128, 512) for _ in range(2)]

        def t3c(pp):
            return (pp // 7) * 512 + (pp % 7) * 65

        for b_ in range(2):
            p.op("vector", lambda e, b_=b_: e.memset(
                vext[b_].rearrange("p (h c) -> p h c", h=16)[:, :, 64:65], 1.0),
                [], [f"vext_{b_}"])

        def pinv_stage0():
            p.op("vector", lambda e: e.memset(x2bd, 0.0), [], ["x2bd"])
            for pp in range(8):
                pv, pvk = ps_pv[pp % 2], f"pspv{pp % 2}"
                sl = slice(pp * 128, (pp + 1) * 128)
                p.mm(pv[:, 0:128], qlbd32[:, sl], klbd32[:, sl],
                     ["qlbd32", "klbd32"], [pvk])
                p.op("scalar", lambda e, pv=pv, pp=pp: e.activation(
                    out=x2bd[0:64, pp * 128:pp * 128 + 64],
                    in_=pv[0:64, 0:64], func=AF.Exp, scale=SCALE),
                    [pvk], ["x2bd"])
                p.op("scalar", lambda e, pv=pv, pp=pp: e.activation(
                    out=x2bd[64:128, pp * 128 + 64:(pp + 1) * 128],
                    in_=pv[64:128, 64:128], func=AF.Exp, scale=SCALE),
                    [pvk], ["x2bd"])
                p.op("vector", lambda e, pp=pp: e.reduce_sum(
                    sden[0:64, 0:1], x2bd[0:64, pp * 128:pp * 128 + 64],
                    axis=AX.X), ["x2bd"], ["sd0"])
                p.op("vector", lambda e, pp=pp: e.reduce_sum(
                    sden[64:128, 0:1],
                    x2bd[64:128, pp * 128 + 64:(pp + 1) * 128],
                    axis=AX.X), ["x2bd"], ["sd0"])
                p.op("vector", lambda e: e.reciprocal(
                    out=sden[:, 1:2], in_=sden[:, 0:1]), ["sd0"], ["sd1"])
                p.op("vector", lambda e, pp=pp: e.tensor_scalar(
                    out=x2bd[0:64, pp * 128:pp * 128 + 64],
                    in0=x2bd[0:64, pp * 128:pp * 128 + 64],
                    scalar1=sden[0:64, 1:2], scalar2=None, op0=ALU.mult),
                    ["x2bd", "sd1"], ["x2bd"])
                p.op("vector", lambda e, pp=pp: e.tensor_scalar(
                    out=x2bd[64:128, pp * 128 + 64:(pp + 1) * 128],
                    in0=x2bd[64:128, pp * 128 + 64:(pp + 1) * 128],
                    scalar1=sden[64:128, 1:2], scalar2=None, op0=ALU.mult),
                    ["x2bd", "sd1"], ["x2bd"])
                p.op("tensor", lambda e, pv=pv, sl=sl: e.transpose(
                    pv[:, 0:128], x2bd[:, sl], eye_sb),
                    ["x2bd", "eye"], [pvk])
                p.op("vector", lambda e, pv=pv, sl=sl: e.tensor_copy(
                    out=x2Tbd[:, sl], in_=pv[:, 0:128]), [pvk], ["x2Tbd"])

        def pinv_iter(it):
            Vin = x2bd if it == 0 else Vbd[(it - 1) % 2]
            VTin = x2Tbd if it == 0 else VTbd[(it - 1) % 2]
            vik = "x2bd" if it == 0 else f"V{(it - 1) % 2}"
            vtik = "x2Tbd" if it == 0 else f"VT{(it - 1) % 2}"
            Vo, VTo = Vbd[it % 2], VTbd[it % 2]
            for pp in range(8):
                sl = slice(pp * 128, (pp + 1) * 128)
                pv0, pk0 = ps_pv[0], "pspv0"
                pv1, pk1 = ps_pv[1], "pspv1"
                p.mm(pv0[:, 0:128], x2Tbd[:, sl], Vin[:, sl],
                     ["x2Tbd", vik], [pk0])
                p.op("vector", lambda e, pv0=pv0, sl=sl: e.tensor_copy(
                    out=kvbd[:, sl], in_=pv0[:, 0:128]), [pk0], [f"kv{pp}"])
                p.mm(pv1[:, 0:128], Vin[:, sl], x2Tbd[:, sl],
                     [vik, "x2Tbd"], [pk1])
                p.op("vector", lambda e, pv1=pv1, sl=sl: e.tensor_copy(
                    out=kvtbd[:, sl], in_=pv1[:, 0:128]), [pk1], [f"kvt{pp}"])
                p.op("vector", lambda e, sl=sl: e.scalar_tensor_tensor(
                    out=bbd[:, sl], in0=eye_sb, scalar=7.0, in1=kvbd[:, sl],
                    op0=ALU.mult, op1=ALU.subtract),
                    ["eye", f"kv{pp}"], [f"b{pp}"])
                p.mm(pv0[:, 0:128], kvtbd[:, sl], bbd[:, sl],
                     [f"kvt{pp}", f"b{pp}"], [pk0])
                p.op("vector", lambda e, pv0=pv0, sl=sl: e.tensor_copy(
                    out=mbd[:, sl], in_=pv0[:, 0:128]), [pk0], [f"m{pp}"])
                p.op("vector", lambda e, sl=sl: e.scalar_tensor_tensor(
                    out=bbd[:, sl], in0=eye_sb, scalar=15.0, in1=mbd[:, sl],
                    op0=ALU.mult, op1=ALU.subtract),
                    ["eye", f"m{pp}"], [f"b{pp}"])
                p.mm(pv1[:, 0:128], kvtbd[:, sl], bbd[:, sl],
                     [f"kvt{pp}", f"b{pp}"], [pk1])
                p.op("vector", lambda e, pv1=pv1, sl=sl: e.tensor_copy(
                    out=mbd[:, sl], in_=pv1[:, 0:128]), [pk1], [f"m{pp}"])
                p.op("vector", lambda e, sl=sl: e.scalar_tensor_tensor(
                    out=bbd[:, sl], in0=eye_sb, scalar=13.0, in1=mbd[:, sl],
                    op0=ALU.mult, op1=ALU.subtract),
                    ["eye", f"m{pp}"], [f"b{pp}"])
                p.mm(pv0[:, 0:128], VTin[:, sl], bbd[:, sl],
                     [vtik, f"b{pp}"], [pk0])
                p.op("vector", lambda e, pv0=pv0, Vo=Vo, sl=sl: e.tensor_scalar(
                    out=Vo[:, sl], in0=pv0[:, 0:128], scalar1=0.25,
                    scalar2=None, op0=ALU.mult), [pk0], [f"V{it % 2}"])
                p.mm(pv1[:, 0:128], bbd[:, sl], VTin[:, sl],
                     [f"b{pp}", vtik], [pk1])
                p.op("vector", lambda e, pv1=pv1, VTo=VTo, sl=sl: e.tensor_scalar(
                    out=VTo[:, sl], in0=pv1[:, 0:128], scalar1=0.25,
                    scalar2=None, op0=ALU.mult), [pk1], [f"VT{it % 2}"])

        if stages >= 2:
            pinv_stage0()
        def B_loadk(g):
            if g >= (N_TOK // 512):
                return
            kt_, ktk_ = kT_sb[g % 2], f"kT_{g % 2}"
            for dc in range(8):
                q_ = "sync" if dc % 2 == 0 else "scalar"
                p.op(q_, lambda e, kt_=kt_, dc=dc, g=g: e.dma_start(
                    out=kt_[:, dc * 512:(dc + 1) * 512],
                    in_=qkv_d[g * 512:(g + 1) * 512,
                              1024 + dc * 128:1024 + (dc + 1) * 128],
                    transpose=True), [], [ktk_], inc=16)

        B_loadk(0)
        for i in range(NT):
            g, jj = i // 4, i % 4
            kt, ktk = kT_sb[g % 2], f"kT_{g % 2}"
            if jj == 0:
                B_loadk(g + 1)
            vx, vxk = vext[i % 2], f"vext_{i % 2}"
            p.dma("scalar",
                  vx.rearrange("p (h c) -> p h c", h=16)[:, :, 0:64],
                  qkv_d[i * 128:(i + 1) * 128, 2048:3072]
                  .rearrange("p (h c) -> p h c", h=16),
                  [], [vxk])
            for pp in range(8):
                p.mm(ps_s3[:, pp * 128:(pp + 1) * 128],
                     kt[:, pp * 512 + jj * 128:pp * 512 + (jj + 1) * 128],
                     qlbd_bf[:, pp * 128:(pp + 1) * 128],
                     [ktk, "qlbd"], ["ps_s3"])
            pb, pbk = p3t[i % 2], f"p3t_{i % 2}"
            p.op("scalar", lambda e, pb=pb: e.activation(
                out=pb, in_=ps_s3, func=AF.Exp, scale=SCALE),
                ["ps_s3"], [pbk])
            for pp in range(8):
                hA, hB = 2 * pp, 2 * pp + 1
                p.mm(ps_t3vE[:, t3c(pp):t3c(pp) + 65],
                     pb[:, hA * 64:(hA + 1) * 64], vx[:, hA * 65:(hA + 1) * 65],
                     [pbk, vxk], ["ps_t3vE"],
                     start=(i == 0 and pp in (0, 7)), stop=(i == NT - 1))
                p.op("tensor", lambda e, pp=pp, pb=pb, vx=vx, i=i: e.matmul(
                    ps_t3vO[64:128, t3c(pp):t3c(pp) + 65],
                    pb[:, (2 * pp + 1) * 64:(2 * pp + 2) * 64],
                    vx[:, (2 * pp + 1) * 65:(2 * pp + 2) * 65],
                    start=(i == 0 and pp in (0, 7)), stop=(i == NT - 1),
                    tile_position=(0, 64)),
                    [pbk, vxk], ["ps_t3vO"])
            if stages >= 3 and i % 8 == 7 and i // 8 < 6:
                pinv_iter(i // 8)

        if stages >= 4:
            p.op("vector", lambda e: e.tensor_copy(
                out=t3vraw[0:64, :], in_=ps_t3vE), ["ps_t3vE"], ["t3vraw"])
            p.op("vector", lambda e: e.tensor_copy(
                out=t3vraw[64:128, :], in_=ps_t3vO[64:128, :]),
                ["ps_t3vO"], ["t3vraw"])
            p.op("vector", lambda e: e.tensor_copy(out=VT_bf, in_=VTbd[1]),
                 ["VT1"], ["VT_bf"])
            p.op("vector", lambda e: e.memset(inner_bd, 0.0), [], ["inner_bd"])
            p.op("vector", lambda e: e.memset(
                inner_bd.rearrange("p (q c) -> p q c", q=8)[0:64, :, 64:65],
                1.0), [], ["inner_bd"])
            p.op("vector", lambda e: e.memset(
                inner_bd.rearrange("p (q c) -> p q c", q=8)[64:128, :, 129:130],
                1.0), [], ["inner_bd"])
            for pp in range(8):
                c = t3c(pp)
                p.op("vector", lambda e, c=c, pp=pp: e.reciprocal(
                    out=sden[:, 2 + pp:3 + pp], in_=t3vraw[:, c + 64:c + 65]),
                    ["t3vraw"], [f"rd{pp}"])
                p.op("vector", lambda e, c=c, pp=pp: e.tensor_scalar(
                    out=t3vn[:, pp * 64:(pp + 1) * 64],
                    in0=t3vraw[:, c:c + 64],
                    scalar1=sden[:, 2 + pp:3 + pp], scalar2=None, op0=ALU.mult),
                    ["t3vraw", f"rd{pp}"], ["t3vn"])
                pv, pvk = ps_pv[pp % 2], f"pspv{pp % 2}"
                p.mm(pv[:, 0:64], VT_bf[:, pp * 128:(pp + 1) * 128],
                     t3vn[:, pp * 64:(pp + 1) * 64],
                     ["VT_bf", "t3vn"], [pvk])
                p.op("vector", lambda e, pv=pv, pp=pp: e.tensor_copy(
                    out=inner_bd[0:64, pp * 130:pp * 130 + 64],
                    in_=pv[0:64, 0:64]), [pvk], ["inner_bd"])
                p.op("vector", lambda e, pv=pv, pp=pp: e.tensor_copy(
                    out=inner_bd[64:128, pp * 130 + 65:pp * 130 + 129],
                    in_=pv[64:128, 0:64]), [pvk], ["inner_bd"])
            if debug:
                p.dma("sync", dbg["attn2"][:, :], x2bd, ["x2bd"], [])
                p.dma("sync", dbg["pinvT"][:, :], VTbd[1], ["VT1"], [])
                p.dma("sync", dbg["t3v"][:, :], t3vraw, ["t3vraw"], [])
                p.dma("sync", dbg["inner"][:, :], inner_bd, ["inner_bd"], [])

        ar.release(markP)
        mark0 = markP
        ps.release(0)
        p.barrier()

        wout_sb = ar.alloc(128, 8 * 1024, BF16)
        bout_rep = ar.alloc(128, 1024, F32)
        p.dma("gpsimd", wout_sb.rearrange("p (c n) -> p c n", c=8),
              wout_d.ap().rearrange("(c p) n -> p c n", p=128),
              [], ["wout"])
        p.dma("sync", bout_rep, bout_d[0:1, :].to_broadcast([128, 1024]),
              [], ["boutr"])
        qT_sb = [ar.alloc(128, 8 * 512, BF16) for _ in range(2)]
        x0t = [ar.alloc(128, 1024, F32) for _ in range(3)]
        p1t = [ar.alloc(128, 1024, BF16) for _ in range(2)]
        attnb = [ar.alloc(128, 1024, BF16) for _ in range(2)]
        attnT = [ar.alloc(128, 1024, BF16) for _ in range(2)]
        attno = ar.alloc(128, 1024, F32)
        ln2f = ar.alloc(128, 1024, F32)
        x1t = [ar.alloc(128, 1024, F32) for _ in range(2)]
        dsc = ar.alloc(128, 4, F32)

        ps_s1 = ps.alloc(128, 1024)
        ps_oh = ps.alloc(128, 1536)
        ps_wo = [ps.alloc(128, 512) for _ in range(2)]

        def ohc(pp):
            return (pp // 3) * 512 + (pp % 3) * 130

        def C1_loadq(g):
            if g >= (N_TOK // 512):
                return
            qt, qtk = qT_sb[g % 2], f"qT_{g % 2}"
            for dc in range(8):
                q_ = "sync" if dc % 2 == 0 else "scalar"
                p.op(q_, lambda e, qt=qt, dc=dc, g=g: e.dma_start(
                    out=qt[:, dc * 512:(dc + 1) * 512],
                    in_=qkv_d[g * 512:(g + 1) * 512,
                              dc * 128:(dc + 1) * 128],
                    transpose=True), [], [qtk], inc=16)

        def C1_s0(i):
            if i % 4 == 0:
                C1_loadq(i // 4 + 1)
            p.dma("scalar", x0t[i % 3], x0_d[i * 128:(i + 1) * 128, :],
                  [], [f"x0c_{i % 3}"])

        def C1_s1(i):
            g, jj = i // 4, i % 4
            qt, qtk = qT_sb[g % 2], f"qT_{g % 2}"
            for pp in range(8):
                p.mm(ps_s1[:, pp * 128:(pp + 1) * 128],
                     klbd_bf[:, pp * 128:(pp + 1) * 128],
                     qt[:, pp * 512 + jj * 128:pp * 512 + (jj + 1) * 128],
                     ["klbd", qtk], ["ps_s1"])
            pb, pbk = p1t[i % 2], f"p1t_{i % 2}"
            p.op("scalar", lambda e, pb=pb: e.activation(
                out=pb, in_=ps_s1, func=AF.Exp, scale=SCALE),
                ["ps_s1"], [pbk])
            for pp in range(8):
                p.mm(ps_oh[:, ohc(pp):ohc(pp) + 130],
                     pb[:, pp * 128:(pp + 1) * 128],
                     inner_bd[:, pp * 130:(pp + 1) * 130],
                     [pbk, "inner_bd"], ["ps_oh"])
            ab, abk = attnb[i % 2], f"attnb_{i % 2}"
            for pp in range(8):
                c = ohc(pp)
                p.op("vector", lambda e, c=c: e.tensor_copy(
                    out=dsc[:, 0:2].unsqueeze(2),
                    in_=ps_oh[:, c:c + 130]
                    .rearrange("p (s c) -> p s c", s=2)[:, :, 64:65]),
                    ["ps_oh"], ["dsc"])
                p.op("vector", lambda e: e.reciprocal(
                    out=dsc[:, 2:4], in_=dsc[:, 0:2]), ["dsc"], ["dsc2"])
                p.op("vector", lambda e, c=c, ab=ab, pp=pp: e.tensor_mul(
                    ab[:, pp * 128:(pp + 1) * 128]
                    .rearrange("p (s c) -> p s c", s=2),
                    ps_oh[:, c:c + 130]
                    .rearrange("p (s c) -> p s c", s=2)[:, :, 0:64],
                    dsc[:, 2:4].unsqueeze(2).to_broadcast([128, 2, 64])),
                    ["ps_oh", "dsc2"], [abk + str(pp // 4)])
            at, atk = attnT[i % 2], f"attnT_{i % 2}"
            p.dmatb(at[:, 0:512], ab[:, 0:512], 4, [abk + "0"], [atk + "a"])
            p.dmatb(at[:, 512:1024], ab[:, 512:1024], 4,
                    [abk + "1"], [atk + "b"])

        def C1_s2(i):
            at, atk = attnT[i % 2], f"attnT_{i % 2}"
            for nch in range(2):
                pw, pwk = ps_wo[nch], f"pswo_{nch}"
                for kc in range(8):
                    p.mm(pw, at[:, kc * 128:(kc + 1) * 128],
                         wout_sb[:, kc * 1024 + nch * 512:kc * 1024 + nch * 512 + 512],
                         [atk + ("a" if kc < 4 else "b"), "wout"], [pwk],
                         start=(kc == 0), stop=(kc == 7))
                p.op("vector", lambda e, pw=pw, nch=nch: e.tensor_add(
                    attno[:, nch * 512:(nch + 1) * 512], pw,
                    bout_rep[:, nch * 512:(nch + 1) * 512]),
                    [pwk, "boutr"], ["attno"])
            rk, nk_ = emit_ln(attno, "attno", "", par=i % 2)
            p.op("vector", lambda e, j=i % 2: e.tensor_scalar(
                out=ln2f, in0=attno, scalar1=LNSx[j]["rstd"],
                scalar2=LNSx[j]["nbias"],
                op0=ALU.mult, op1=ALU.add), ["attno", rk, nk_], ["ln2f"])
            x1, x1k = x1t[i % 2], f"x1_{i % 2}"
            p.op("vector", lambda e, x1=x1, i=i: e.tensor_add(
                x1, ln2f, x0t[i % 3]),
                ["ln2f", f"x0c_{i % 3}"], [x1k])
            p.dma("gpsimd", x1_d[i * 128:(i + 1) * 128, :], x1, [x1k], [])

        C1_loadq(0)
        for i in range(NT + 2):
            if 1 <= i and i - 1 < NT:
                C1_s1(i - 1)
            if 2 <= i:
                C1_s2(i - 2)
            if i < NT:
                C1_s0(i)

        ar.release(mark0)
        ps.release(0)
        p.barrier()

        wup_sb = ar.alloc(128, 8 * 4096, BF16)
        wdn_sb = ar.alloc(128, 32 * 1024, BF16)
        wq_sb = ar.alloc(128, 8 * 512, BF16)
        keysbd_sb = ar.alloc(128, 512, F32)
        p.dma("gpsimd", wup_sb.rearrange("p (c n) -> p c n", c=8),
              wup_d.ap().rearrange("(c p) n -> p c n", p=128), [], ["wup"])
        p.dma("gpsimd", wdn_sb.rearrange("p (c n) -> p c n", c=32),
              wdown_d.ap().rearrange("(c p) n -> p c n", p=128), [], ["wdn"])
        p.dma("gpsimd", wq_sb.rearrange("p (c n) -> p c n", c=8),
              wq_d.ap().rearrange("(c p) n -> p c n", p=128), [], ["wq"])
        p.dma("sync", keysbd_sb, keysbd_d[:, :], [], ["keysbd"])

        x1t2 = [ar.alloc(128, 1024, F32) for _ in range(3)]
        ln3b = ar.alloc(128, 1024, BF16)
        ln3T = [ar.alloc(128, 1024, BF16) for _ in range(3)]
        u_bf = ar.alloc(128, 4096, BF16)
        uT = ar.alloc(128, 4096, BF16)
        th_sc = [ar.alloc(128, 512, BF16) for _ in range(2)]
        dense_sb = ar.alloc(128, 1024, F32)
        qfT_sb = ar.alloc(128, 512, F32)
        sim_sb = ar.alloc(128, 512, F32)
        mx8 = ar.alloc(128, 64, F32)
        msk = ar.alloc(128, 512, F32)
        ind_bf = ar.alloc(128, 512, BF16)
        eall = ar.alloc(128, 512, BF16)
        exm_sb = ar.alloc(128, 512, BF16)
        emx = ar.alloc(128, 64, BF16)
        candp = ar.alloc(128, 256, BF16)
        scp8 = ar.alloc(128, 32, BF16)
        zt = ar.alloc(128, 4, F32)
        selt = ar.alloc(128, 16, F32)

        ps_qf = ps.alloc(128, 512)
        ps_sim = ps.alloc(128, 512)
        ps_up = [ps.alloc(128, 512) for _ in range(4)]
        ps_dn = [ps.alloc(128, 512) for _ in range(2)]

        def C2_s0(i):
            p.dma("sync", x1t2[i % 3], x1_d[i * 128:(i + 1) * 128, :],
                  [], [f"x1c2_{i % 3}"])

        def C2_s1(i):
            x1, x1k = x1t2[i % 3], f"x1c2_{i % 3}"
            rk, nk_ = emit_ln(x1, x1k, "", par=i % 2)
            p.op("vector", lambda e, x1=x1, j=i % 2: e.tensor_scalar(
                out=ln3b, in0=x1, scalar1=LNSx[j]["rstd"],
                scalar2=LNSx[j]["nbias"],
                op0=ALU.mult, op1=ALU.add), [x1k, rk, nk_], ["ln3b"])
            lT, lTk = ln3T[i % 3], f"ln3T_{i % 3}"
            p.dmatb(lT, ln3b, 8, ["ln3b"], [lTk])

        def C2_s2(i):
            lT, lTk = ln3T[i % 3], f"ln3T_{i % 3}"
            for nch in range(8):
                pu, puk = ps_up[nch % 4], f"psup_{nch % 4}"
                for dc in range(8):
                    p.mm(pu, lT[:, dc * 128:(dc + 1) * 128],
                         wup_sb[:, dc * 4096 + nch * 512:dc * 4096 + nch * 512 + 512],
                         [lTk, "wup"], [puk],
                         start=(dc == 0), stop=(dc == 7))
                th = th_sc[nch % 2]
                thk = f"th_{nch % 2}"
                p.op("scalar", lambda e, pu=pu, th=th: e.activation(
                    out=th, in_=pu, func=AF.Tanh, scale=0.5),
                    [puk], [thk])
                p.op("vector", lambda e, th=th: e.tensor_scalar(
                    out=th, in0=th, scalar1=0.5, scalar2=0.5,
                    op0=ALU.mult, op1=ALU.add), [thk], [thk])
                p.op("vector", lambda e, pu=pu, nch=nch, th=th: e.tensor_mul(
                    u_bf[:, nch * 512:(nch + 1) * 512], th, pu),
                    [thk, puk], [f"u_{nch // 4}"])
            p.dmatb(uT[:, 0:2048], u_bf[:, 0:2048], 16, ["u_0"], ["uTa"])
            for c in range(4):
                for dc in range(8):
                    p.mm(ps_qf[:, c * 128:(c + 1) * 128],
                         wq_sb[:, dc * 512 + c * 128:dc * 512 + (c + 1) * 128],
                         lT[:, dc * 128:(dc + 1) * 128],
                         ["wq", lTk], ["ps_qf"],
                         start=(dc == 0), stop=(dc == 7))
            p.dmatb(uT[:, 2048:4096], u_bf[:, 2048:4096], 16,
                    ["u_1"], ["uTb"])
            p.op("scalar", lambda e: e.copy(out=qfT_sb, in_=ps_qf),
                 ["ps_qf"], ["qfT"])
            for c in range(4):
                p.mm(ps_sim[:, c * 128:(c + 1) * 128],
                     qfT_sb[:, c * 128:(c + 1) * 128],
                     keysbd_sb[:, c * 128:(c + 1) * 128],
                     ["qfT", "keysbd"], ["ps_sim"])
            p.op("scalar", lambda e: e.copy(out=sim_sb, in_=ps_sim),
                 ["ps_sim"], ["sim"])
            for nch in range(2):
                pd, pdk = ps_dn[nch], f"psdn_{nch}"
                for kc in range(32):
                    p.mm(pd, uT[:, kc * 128:(kc + 1) * 128],
                         wdn_sb[:, kc * 1024 + nch * 512:kc * 1024 + nch * 512 + 512],
                         ["uTa" if kc < 16 else "uTb", "wdn"], [pdk],
                         start=(kc == 0), stop=(kc == 31))
                p.op("scalar", lambda e, pd=pd, nch=nch: e.copy(
                    out=dense_sb[:, nch * 512:(nch + 1) * 512], in_=pd),
                    [pdk], ["dense_sb"])
            p.dma("gpsimd", dense_d[i * 128:(i + 1) * 128, :], dense_sb,
                  ["dense_sb"], [])
        def C2_s3(i):
            for gI in range(8):
                p.op("vector", lambda e, gI=gI: e.max(
                    out=mx8[:, gI * 8:(gI + 1) * 8],
                    in_=sim_sb[:, gI * 64:(gI + 1) * 64]), ["sim"], ["mx8"])
            for gI in range(8):
                p.op("vector", lambda e, gI=gI: e.match_replace(
                    out=msk[:, gI * 64:(gI + 1) * 64],
                    in_to_replace=mx8[:, gI * 8:(gI + 1) * 8],
                    in_values=sim_sb[:, gI * 64:(gI + 1) * 64],
                    imm_value=-BIG), ["mx8", "sim"], ["msk"])
            p.op("vector", lambda e: e.tensor_scalar(
                out=ind_bf, in0=msk, scalar1=-BIG / 2, scalar2=None,
                op0=ALU.is_le), ["msk"], ["ind"])
            p.op("scalar", lambda e: e.activation(
                out=eall, in_=sim_sb, func=AF.Exp), ["sim"], ["eall"])
            p.op("vector", lambda e: e.tensor_mul(exm_sb, eall, ind_bf),
                 ["eall", "ind"], ["exm"])
            p.dma("gpsimd", exm_d[i * 128:(i + 1) * 128, :], exm_sb,
                  ["exm"], [])
            p.op("scalar", lambda e: e.activation(
                out=emx, in_=mx8, func=AF.Exp), ["mx8"], ["emx"])
            p.op("vector", lambda e: e.tensor_mul(
                candp.rearrange("p (h a b) -> p h a b", h=4, a=8),
                emx[:, 0:32].rearrange("p (h a) -> p h a", h=4)
                    .unsqueeze(3).to_broadcast([128, 4, 8, 8]),
                emx[:, 32:64].rearrange("p (h b) -> p h b", h=4)
                    .unsqueeze(2).to_broadcast([128, 4, 8, 8])),
                ["emx"], ["candp"])
            for h in range(4):
                p.op("vector", lambda e, h=h: e.max(
                    out=scp8[:, h * 8:(h + 1) * 8],
                    in_=candp[:, h * 64:(h + 1) * 64]), ["candp"], ["scp8"])
            p.op("vector", lambda e: e.tensor_copy(
                out=selt[:, 0:4].unsqueeze(2),
                in_=scp8.rearrange("p (h k) -> p h k", h=4)[:, :, 7:8]),
                ["scp8"], ["selt"])
            p.op("vector", lambda e: e.reduce_sum(
                zt.unsqueeze(2), scp8.rearrange("p (h k) -> p h k", h=4),
                axis=AX.X), ["scp8"], ["zt"])
            p.op("vector", lambda e: e.reciprocal(
                out=selt[:, 4:8], in_=zt), ["zt"], ["selt"])
            p.dma("gpsimd", sel_d[i * 128:(i + 1) * 128, :], selt,
                  ["selt"], [])

        for i in range(NT + 3):
            if 3 <= i:
                C2_s2(i - 3)
            if 1 <= i and i - 1 < NT:
                C2_s1(i - 1)
            if 3 <= i:
                C2_s3(i - 3)
            if i < NT:
                C2_s0(i)

        ar.release(mark0)
        ps.release(0)
        p.barrier()

        de_sb = ar.alloc(128, 8 * 4096, BF16)
        ue_sb = ar.alloc(128, 32 * 1024, BF16)
        p.dma("gpsimd", de_sb.rearrange("p (c n) -> p c n", c=8),
              de_d.ap().rearrange("(c p) n -> p c n", p=128), [], ["de"])
        p.dma("gpsimd", ue_sb.rearrange("p (c n) -> p c n", c=32),
              ue_d.ap().rearrange("(c p) n -> p c n", p=128), [], ["ue"])

        x1f = [ar.alloc(128, 1024, F32) for _ in range(2)]
        densef = ar.alloc(128, 1024, F32)
        x1b = ar.alloc(128, 1024, F32)
        exmf = [ar.alloc(128, 512, BF16) for _ in range(2)]
        self_t = [ar.alloc(128, 16, F32) for _ in range(2)]
        ln3b2 = ar.alloc(128, 1024, BF16)
        ln3T2 = [ar.alloc(128, 1024, BF16) for _ in range(2)]
        exr = [ar.alloc(128, 256, BF16) for _ in range(2)]
        taur = [ar.alloc(128, 4, F32) for _ in range(2)]
        dots_bf = [ar.alloc(128, 4096, BF16) for _ in range(3)]
        G_bf = ar.alloc(128, 4096, BF16)
        md_bf = ar.alloc(128, 4096, BF16)
        wacc = ar.alloc(128, 4096, BF16)


        ps_dt = [ps.alloc(128, 512) for _ in range(4)]
        ps_x = ps.alloc(128, 1024)

        def F_s0(i):
            bi = i % 2
            p.dma("scalar", x1f[bi], x1_d[i * 128:(i + 1) * 128, :],
                  [], [f"x1f_{bi}"])
            p.dma("scalar", exmf[bi], exm_d[i * 128:(i + 1) * 128, :],
                  [], [f"exmf_{bi}"])
            p.dma("scalar", self_t[bi], sel_d[i * 128:(i + 1) * 128, :],
                  [], [f"self_{bi}"])

        def F_s1(i):
            bi = i % 2
            ti = i % 3
            x1, x1k = x1f[bi], f"x1f_{bi}"
            ex, exk = exmf[bi], f"exmf_{bi}"
            se, sek = self_t[bi], f"self_{bi}"
            rk, nk_ = emit_ln(x1, x1k, "", par=bi)
            p.op("vector", lambda e, x1=x1, bi=bi: e.tensor_scalar(
                out=ln3b2, in0=x1, scalar1=LNSx[bi]["rstd"],
                scalar2=LNSx[bi]["nbias"],
                op0=ALU.mult, op1=ALU.add), [x1k, rk, nk_], ["ln3b2"])
            lT, lTk = ln3T2[bi], f"ln3T2_{bi}"
            p.dmatb(lT, ln3b2, 8, ["ln3b2"], [lTk])
            p.op("vector", lambda e, se=se, bi=bi: e.tensor_mul(
                taur[bi], se[:, 0:4], se[:, 4:8]), [sek], [f"taur_{bi}"])
            for h in range(FH):
                p.op("vector", lambda e, ex=ex, se=se, h=h, bi=bi: e.tensor_scalar(
                    out=exr[bi][:, h * 64:(h + 1) * 64],
                    in0=ex[:, h * 64:(h + 1) * 64],
                    scalar1=se[:, 4 + h:5 + h], scalar2=None, op0=ALU.mult),
                    [exk, sek], [f"exr_{bi}"])
            dot = dots_bf[ti]
            for ec in range(8):
                pd, pdk = ps_dt[ec % 4], f"psdt_{ec % 4}"
                for dc in range(8):
                    p.mm(pd, lT[:, dc * 128:(dc + 1) * 128],
                         de_sb[:, dc * 4096 + ec * 512:dc * 4096 + ec * 512 + 512],
                         [lTk, "de"], [pdk],
                         start=(dc == 0), stop=(dc == 7))
                if ec % 4 == 3:
                    half = ec // 4
                    p.op("scalar", lambda e, half=half, dot=dot: e.copy(
                        out=dot[:, half * 2048:(half + 1) * 2048],
                        in_=bass.AP(ps_dt[0].tensor, ps_dt[0].offset,
                                    [ps_dt[0].ap[0], [1, 2048]])),
                        [f"psdt_{k}" for k in range(4)],
                        [f"dots_{ti}{half}"])

        def F_s2(i):
            bi = i % 2
            ti = i % 3
            ex, exk = exmf[bi], f"exmf_{bi}"
            dot = dots_bf[ti]
            def unitA(u):
                h, q = u // 2, u % 2
                eyh = ex[:, 256 + h * 64:256 + (h + 1) * 64]
                rows = slice(q * 32, (q + 1) * 32)
                csl = slice(q * 2048, (q + 1) * 2048)
                gk, mk = f"G{q}", f"md{q}"
                Gv, mdv = G_bf[:, csl], md_bf[:, csl]
                p.op("vector", lambda e, eyh=eyh, rows=rows, Gv=Gv, h=h:
                     e.tensor_mul(
                    Gv.rearrange("p (a b) -> p a b", a=32),
                    exr[bi][:, h * 64:(h + 1) * 64][:, rows]
                    .unsqueeze(2).to_broadcast([128, 32, 64]),
                    eyh.unsqueeze(1).to_broadcast([128, 32, 64])),
                    [f"exr_{bi}", exk], [gk])
                p.op("vector", lambda e, h=h, Gv=Gv, mdv=mdv: e.tensor_scalar(
                    out=mdv, in0=Gv, scalar1=taur[bi][:, h:h + 1],
                    scalar2=None, op0=ALU.is_ge),
                    [gk, f"taur_{bi}"], [mk])
                p.op("vector", lambda e, mdv=mdv, csl=csl: e.tensor_mul(
                    mdv, mdv, dot[:, csl]), [mk, f"dots_{ti}{q}"], [mk])
                p.op("vector", lambda e, Gv=Gv, mdv=mdv: e.tensor_mul(
                    mdv, Gv, mdv), [gk, mk], [mk])
                p.op("scalar", lambda e, Gv=Gv, mdv=mdv: e.activation(
                    out=Gv, in_=mdv, func=AF.Tanh, scale=0.5), [mk], [gk])
                p.op("scalar", lambda e, Gv=Gv: e.activation(
                    out=Gv, in_=Gv, func=AF.Copy, scale=0.5, bias=0.5),
                    [gk], [gk])

            def unitB(u):
                h, q = u // 2, u % 2
                csl = slice(q * 2048, (q + 1) * 2048)
                gk, mk = f"G{q}", f"md{q}"
                Gv, mdv = G_bf[:, csl], md_bf[:, csl]
                p.op("vector", lambda e, Gv=Gv, mdv=mdv: e.tensor_mul(
                    mdv, Gv, mdv), [gk, mk], [mk])
                if h == 0:
                    p.op("vector", lambda e, mdv=mdv, csl=csl: e.tensor_copy(
                        out=wacc[:, csl], in_=mdv), [mk], [f"wacc{q}"])
                else:
                    p.op("vector", lambda e, mdv=mdv, csl=csl: e.tensor_add(
                        wacc[:, csl], wacc[:, csl], mdv),
                        [mk, f"wacc{q}"], [f"wacc{q}"])

            for u in range(9):
                if u < 8:
                    unitA(u)
                if u >= 1:
                    unitB(u - 1)
            p.dmatb(dot[:, 0:2048], wacc[:, 0:2048], 16,
                    ["wacc0"], [f"dots_{ti}0"])
            p.dmatb(dot[:, 2048:4096], wacc[:, 2048:4096], 16,
                    ["wacc1"], [f"dots_{ti}1"])

        def F_s3(i):
            bi = i % 2
            ti = i % 3
            dot = dots_bf[ti]
            dotk = [f"dots_{ti}0", f"dots_{ti}1"]
            p.dma("scalar", densef, dense_d[i * 128:(i + 1) * 128, :],
                  [], ["densef"])
            p.dma("scalar", x1b, x1_d[i * 128:(i + 1) * 128, :], [], ["x1b"])
            for nch in range(2):
                for kc in range(32):
                    p.mm(ps_x[:, nch * 512:(nch + 1) * 512],
                         dot[:, kc * 128:(kc + 1) * 128],
                         ue_sb[:, kc * 1024 + nch * 512:kc * 1024 + nch * 512 + 512],
                         [dotk[kc // 16], "ue"], ["ps_x"],
                         start=(kc == 0), stop=(kc == 31))
            p.op("vector", lambda e: e.tensor_add(densef, ps_x, densef),
                 ["ps_x", "densef"], ["densef"])
            rk2, nk2 = emit_ln(densef, "densef", "", par=bi)
            p.op("vector", lambda e, bi=bi: e.tensor_scalar(
                out=densef, in0=densef, scalar1=LNSx[bi]["rstd"],
                scalar2=LNSx[bi]["nbias"],
                op0=ALU.mult, op1=ALU.add), ["densef", rk2, nk2], ["densef"])
            p.op("vector", lambda e: e.tensor_add(densef, densef, x1b),
                 ["densef", "x1b"], ["densef"])
            p.dma("gpsimd", out_d[i * 128:(i + 1) * 128, :], densef,
                  ["densef"], [])

        for i in range(NT + 3):
            if 2 <= i and i - 2 < NT:
                F_s2(i - 2)
            if 3 <= i:
                F_s3(i - 3)
            if 1 <= i and i - 1 < NT:
                F_s1(i - 1)
            if i < NT:
                F_s0(i)

        ar.release(mark0)
        ps.release(0)
        p.barrier()

        with nc.Block() as block:
            p.emit(block, sems)
    return nc, dbg


def kernel(**inputs):
    shared, per_core = host_prepare(inputs)
    nc, _ = build(debug=False)
    in_maps = [dict(shared, **pc) for pc in per_core]
    res = run_bass_kernel_spmd(nc, in_maps, core_ids=list(range(N_CORES)))
    out = np.stack([np.asarray(r["out"]) for r in res.results], axis=0)
    return out.astype(np.float32)
